# version 11
# baseline (speedup 1.0000x reference)
import numpy as np
import ml_dtypes
import concourse.bass as bass
import concourse.mybir as mybir
from concourse.bass_utils import run_bass_kernel_spmd
from contextlib import ExitStack

F32 = mybir.dt.float32
BF16 = mybir.dt.bfloat16
AF = mybir.ActivationFunctionType
ALU = mybir.AluOpType

B, T, D, H, DK, DV, HID = 4, 8192, 2048, 8, 256, 512, 8192
TP, TM = 4096, 4096
TS = TP + TM
TILE, CH = 1024, 256
NDT = D // 128
NTIL = TS // TILE
NPRE = TP // TILE
NCH = TILE // CH
NFT = TM // TILE
HT = HID // 128
EPS = 1e-5
GAM = [1.0 - 2.0 ** (-5.0 - h) for h in range(H)]
QSC = DK ** -0.5


class Sem:
    __slots__ = ("name", "h", "n")
    def __init__(self, name):
        self.name, self.h, self.n = name, None, 0


class Plan:
    ENGS = ("te", "ve", "ac", "po", "sy")
    def __init__(self):
        self.q = {e: [] for e in self.ENGS}
        self.waited = {e: {} for e in self.ENGS}
        self.sems = {}
        self.paranoid = False
        self.selfpar = ()
        self.cpar = ()

    def sem(self, name):
        if name not in self.sems:
            self.sems[name] = Sem(name)
        return self.sems[name]

    def emit(self, eng, fn):
        if self.paranoid and (self.paranoid is True or eng in self.paranoid):
            for s in self.sems.values():
                self.wait(eng, s)
        elif self.cpar and eng in self.cpar:
            for sn in ("te", "ve", "ac", "po"):
                self.wait(eng, self.sems[sn])
        elif self.selfpar and eng in self.selfpar and eng in ("ve", "po", "ac", "te"):
            self.wait(eng, self.sems[eng])
        self.q[eng].append(fn)

    def wait(self, eng, sem, cnt=None):
        if cnt is None:
            cnt = sem.n
        if cnt <= 0:
            return
        w = self.waited[eng]
        if w.get(sem.name, -1) >= cnt:
            return
        w[sem.name] = cnt
        self.q[eng].append(lambda e, s=sem, c=cnt: e.wait_ge(s.h, c))


def ts_(i, n):
    return slice(i * n, (i + 1) * n)


def build_program(debug=False, paranoid=False, selfpar=(), cpar=()):
    nc = bass.Bass("TRN2", target_bir_lowering=False)
    P = Plan()
    P.paranoid = paranoid
    P.selfpar = selfpar
    P.cpar = cpar
    dbg_kind = "ExternalOutput" if debug else "Internal"

    x_tm = nc.dram_tensor("x_tm", [TS, D], F32, kind="ExternalInput").ap()
    x_fm = nc.dram_tensor("x_fm", [D, TM], F32, kind="ExternalInput").ap()
    wq = nc.dram_tensor("wq", [D, H * DK], F32, kind="ExternalInput").ap()
    wk = nc.dram_tensor("wk", [D, H * DK], F32, kind="ExternalInput").ap()
    wv = nc.dram_tensor("wv", [D, H * DV], F32, kind="ExternalInput").ap()
    wg = nc.dram_tensor("wg", [D, H * DV], F32, kind="ExternalInput").ap()
    wo = nc.dram_tensor("wo", [H * DV, D], F32, kind="ExternalInput").ap()
    wgt = nc.dram_tensor("wgt", [D, HID], F32, kind="ExternalInput").ap()
    wut = nc.dram_tensor("wut", [D, HID], F32, kind="ExternalInput").ap()
    wdt = nc.dram_tensor("wdt", [HID, D], F32, kind="ExternalInput").ap()
    rc_t = nc.dram_tensor("ropec", [128, TS], F32, kind="ExternalInput").ap()
    rs_t = nc.dram_tensor("ropes", [128, TS], F32, kind="ExternalInput").ap()
    dtab = nc.dram_tensor("dtab", [128, H * 2 * CH], BF16, kind="ExternalInput").ap()
    dqt = nc.dram_tensor("dqt", [128, H * CH], BF16, kind="ExternalInput").ap()
    dkt = nc.dram_tensor("dkt", [128, H * CH], BF16, kind="ExternalInput").ap()
    gnwc = nc.dram_tensor("gnwc", [128, DV // 128], F32, kind="ExternalInput").ap()
    maskc = nc.dram_tensor("maskc", [128, 1], F32, kind="ExternalInput").ap()
    identt = nc.dram_tensor("identt", [128, 128], BF16, kind="ExternalInput").ap()

    out_fm = nc.dram_tensor("out_fm", [D, TM], F32, kind="ExternalOutput").ap()

    xn_h = nc.dram_tensor("xn_h", [TS, D], BF16, kind=dbg_kind).ap()
    og_h = nc.dram_tensor("og_h", [H, TM, DV], BF16, kind=dbg_kind).ap()
    at_h = nc.dram_tensor("at_h", [D, TM], F32, kind=dbg_kind).ap()
    wo_bf = nc.dram_tensor("wo_bf", [H * DV, D], BF16, kind=dbg_kind).ap()
    wgu_bl = nc.dram_tensor("wgu_bl", [HT, 128, 4096], BF16, kind=dbg_kind).ap()
    wd_bl = nc.dram_tensor("wd_bl", [NDT, 128, HT * 128], BF16, kind=dbg_kind).ap()
    hid_dbg = (nc.dram_tensor("hid_dbg", [128, 65536], BF16, kind="ExternalOutput").ap()
               if debug else None)

    with ExitStack() as ctx:
        E = ctx.enter_context

        ARB = E(nc.sbuf_tensor("arb", [128, 92160], BF16))
        ARF = E(nc.sbuf_tensor("arf", [128, 5632], F32))
        XNS = ARB[:, 90112:92160]
        CSTG = ARB[:, 90112:92160]
        def wq_v(pb): return ARB[:, pb * 4096:(pb + 1) * 4096]
        def wk_v(pb): return ARB[:, 8192 + pb * 4096:8192 + (pb + 1) * 4096]
        WV_V = ARB[:, 16384:24576]
        WG_V = ARB[:, 24576:32768]
        def xnt_v(pb): return ARB[:, 32768 + pb * 16384:32768 + (pb + 1) * 16384]
        OGB = ARB[:, 65536:69632]
        VT = ARB[:, 69632:73728]
        GS = ARB[:, 73728:77824]
        QP = ARB[:, 77824:79872]
        KP = ARB[:, 79872:81920]
        QDC = ARB[:, 81920:82432]
        KDC = ARB[:, 82432:82944]
        def sbf_v(pb): return ARB[:, 82944 + pb * 1024:82944 + (pb + 1) * 1024]
        def dth_v(pb): return ARB[:, 85504 + pb * 512:85504 + (pb + 1) * 512]
        def dqh_v(pb): return ARB[:, 86528 + pb * 256:86528 + (pb + 1) * 256]
        def dkh_v(pb): return ARB[:, 87040 + pb * 256:87040 + (pb + 1) * 256]
        OGT = ARB[:, 0:32768]
        WOH = ARB[:, 32768:65536]
        HIDB = ARB[:, 0:65536]
        def xnf_v(tc): return ARB[:, 65536 + tc * 8192:65536 + (tc + 1) * 8192]
        def wdb_v(pb): return ARB[:, 81920 + pb * 4096:81920 + (pb + 1) * 4096]

        XS = ARF[:, 0:2048]
        RT = ARF[:, 2048:3072]
        COS = ARF[:, 3072:3584]
        SIN = ARF[:, 3584:4096]
        SF = ARF[:, 4608:5632]
        FST2 = ARF[:, 0:2048]
        RT4 = ARF[:, 2048:4096]
        FST4 = ARF[:, 0:2048]

        bn6 = E(nc.sbuf_tensor("bn6", [128, 24], F32))
        mcol = E(nc.sbuf_tensor("mcol", [128, 2], F32))
        epsc = E(nc.sbuf_tensor("epsc", [128, 1], F32))
        ms_s = E(nc.sbuf_tensor("ms_s", [128, 2 * NCH], F32))
        rstd_s = E(nc.sbuf_tensor("rstd_s", [128, 2 * NCH], F32))
        gnw_s = E(nc.sbuf_tensor("gnw_s", [128, DV // 128], F32))
        mask_s = E(nc.sbuf_tensor("mask_s", [128, 1], F32))
        id_s = E(nc.sbuf_tensor("id_s", [128, 128], BF16))
        sgq = E(nc.sbuf_tensor("sgq", [128, 512], BF16))
        mt_s = E(nc.sbuf_tensor("mt_s", [128, 512], BF16))
        kdT_s = E(nc.sbuf_tensor("kdT_s", [128, 512], BF16))
        ps_f = E(nc.psum_tensor("ps_f", [128, 7 * 512], F32))
        ps_b = E(nc.psum_tensor("ps_b", [128, 1024], BF16))

        def pf(bank, w=512):
            return ps_f[:, bank * 512:bank * 512 + w]

        for s in ("xwA", "xwB", "xnw", "xnb", "xtA", "xtB", "xt2A", "xt2B",
                  "wldA", "wldB", "wldV", "cld", "wcw", "ogw", "ogt", "wo2",
                  "fwA", "fwB", "tab", "tabh", "rcs", "atwA", "atwB", "fr",
                  "outw", "te", "ve", "ac", "po"):
            P.sem(s)
        S = P.sems

        def INC(sem, k=1):
            sem.n += k
            return sem.n

        def _tabs(s):
            s.dma_start(out=gnw_s[:], in_=gnwc).then_inc(S["tab"].h, 16)
            s.dma_start(out=mask_s[:], in_=maskc).then_inc(S["tab"].h, 16)
            s.dma_start(out=id_s[:], in_=identt).then_inc(S["tab"].h, 16)
        P.emit("sy", _tabs)
        INC(S["tab"], 48)
        TABN = S["tab"].n

        P.emit("ve", lambda v: v.memset(epsc[:], float(EPS)).then_inc(S["ve"].h, 1))
        INC(S["ve"])
        EPS_VE = S["ve"].n
        NS = TS // 128
        ve_xn = {}
        xnw_cnt = {}
        for i in range(NS):
            xw = S["xwA"] if i % 2 == 0 else S["xwB"]
            if i >= 1:
                P.wait("sy", S["ve"], ve_xn[i - 1])
            P.emit("sy", lambda s, i=i, xw=xw: s.dma_start(
                out=XS[:], in_=x_tm[ts_(i, 128), :]).then_inc(xw.h, 16))
            INC(xw, 16)
            P.wait("ve", xw)
            P.wait("ve", S["ve"])
            P.emit("ve", lambda v: (
                v.bn_stats(bn6[:, 0:6], XS[:, 0:512]),
                v.bn_stats(bn6[:, 6:12], XS[:, 512:1024]),
                v.bn_stats(bn6[:, 12:18], XS[:, 1024:1536]),
                v.bn_stats(bn6[:, 18:24], XS[:, 1536:2048]).then_inc(S["ve"].h, 1)))
            INC(S["ve"])
            P.wait("ve", S["ve"])
            P.emit("ve", lambda v: v.bn_aggr(mcol[:], bn6[:]).then_inc(S["ve"].h, 1))
            INC(S["ve"])
            P.wait("ac", S["ve"])
            P.emit("ac", lambda a: a.activation(
                mcol[:, 1:2], mcol[:, 1:2], AF.Sqrt, bias=epsc[:, 0:1]).then_inc(S["ac"].h, 1))
            INC(S["ac"])
            P.wait("ve", S["ac"])
            P.emit("ve", lambda v: v.reciprocal(mcol[:, 1:2], mcol[:, 1:2]).then_inc(S["ve"].h, 1))
            INC(S["ve"])
            P.wait("ve", S["ve"])
            if i >= 1:
                P.wait("ve", S["xnw"], xnw_cnt[i - 1])
            P.emit("ve", lambda v: v.tensor_scalar(
                out=XNS[:], in0=XS[:], scalar1=mcol[:, 0:1], scalar2=mcol[:, 1:2],
                op0=ALU.subtract, op1=ALU.mult).then_inc(S["ve"].h, 1))
            INC(S["ve"])
            ve_xn[i] = S["ve"].n
            P.wait("sy", S["ve"], ve_xn[i])
            P.emit("sy", lambda s, i=i: s.dma_start(
                out=xn_h[ts_(i, 128), :], in_=XNS[:]).then_inc(S["xnw"].h, 16))
            INC(S["xnw"], 16)
            xnw_cnt[i] = S["xnw"].n
            if (i + 1) % 8 == 0:
                P.emit("sy", lambda s, cnt=S["xnw"].n: (
                    s.wait_ge(S["xnw"].h, cnt),
                    s.sem_inc(S["xnb"].h, 1)))
                INC(S["xnb"], 1)
        XNW_ALL = S["xnw"].n

        cache_jobs = []
        for r in range(H * DV // 128):
            cache_jobs.append(("plain", wo, wo_bf[ts_(r, 128), :], r, 0, D))
        for r in range(NDT):
            for c0 in range(0, HID, 2048):
                h0 = c0 // 128
                cache_jobs.append(("blk", wgt,
                                   wgu_bl[h0:h0 + 16, :, r * 256:r * 256 + 128].rearrange("h p c -> p h c"),
                                   r, c0, 2048))
                cache_jobs.append(("blk", wut,
                                   wgu_bl[h0:h0 + 16, :, r * 256 + 128:r * 256 + 256].rearrange("h p c -> p h c"),
                                   r, c0, 2048))
        for hk in range(HT):
            cache_jobs.append(("wd", wdt, None, hk, 0, 2048))
        cache_pos = [0]
        cache_started = [False]

        def emit_cache_jobs(njobs):
            i0 = cache_pos[0]
            for (kind, src, dst_ap, r, c0, cw) in cache_jobs[i0:i0 + njobs]:
                if not cache_started[0]:
                    P.wait("po", S["xnw"], XNW_ALL)
                    cache_started[0] = True
                P.wait("po", S["wcw"])
                P.emit("po", lambda g, src=src, r=r, c0=c0, cw=cw: g.dma_start(
                    out=CSTG[:, 0:cw], in_=src[ts_(r, 128), c0:c0 + cw]).then_inc(S["cld"].h, 16))
                INC(S["cld"], 16)
                P.wait("po", S["cld"])
                if kind == "plain":
                    P.emit("po", lambda a, dst_ap=dst_ap, cw=cw: a.dma_start(
                        out=dst_ap, in_=CSTG[:, 0:cw]).then_inc(S["wcw"].h, 16))
                elif kind == "blk":
                    P.emit("po", lambda a, dst_ap=dst_ap: a.dma_start(
                        out=dst_ap,
                        in_=CSTG[:, 0:2048].rearrange("p (h c) -> p h c", c=128)).then_inc(S["wcw"].h, 16))
                else:
                    P.emit("po", lambda a, hk=r: a.dma_start(
                        out=wd_bl[:, :, ts_(hk, 128)].rearrange("i p c -> p i c"),
                        in_=CSTG[:, 0:2048].rearrange("p (i c) -> p i c", c=128)).then_inc(S["wcw"].h, 16))
                INC(S["wcw"], 16)
            cache_pos[0] = min(len(cache_jobs), i0 + njobs)

        def load_head_qk(h, war_te):
            pb = h % 2
            wl = S["wldA"] if pb == 0 else S["wldB"]
            P.wait("po", S["te"], war_te)
            for w_ap, dstf in ((wq, wq_v), (wk, wk_v)):
                for i0 in range(0, NDT, 4):
                    P.emit("po", lambda g, w_ap=w_ap, dstf=dstf, h=h, pb=pb, i0=i0, wl=wl: g.dma_start(
                        out=dstf(pb).rearrange("p (i c) -> p i c", c=256)[:, i0:i0 + 4, :],
                        in_=w_ap[:, ts_(h, DK)].rearrange("(i p) c -> p i c", p=128)[:, i0:i0 + 4, :],
                    ).then_inc(wl.h, 16))
                    INC(wl, 16)

        def load_head_vg(h, war_te):
            P.wait("po", S["te"], war_te)
            for w_ap, dst in ((wv, WV_V), (wg, WG_V)):
                for i0 in range(0, NDT, 4):
                    P.emit("po", lambda g, w_ap=w_ap, dst=dst, h=h, i0=i0: g.dma_start(
                        out=dst.rearrange("p (i c) -> p i c", c=512)[:, i0:i0 + 4, :],
                        in_=w_ap[:, ts_(h, DV)].rearrange("(i p) c -> p i c", p=128)[:, i0:i0 + 4, :],
                    ).then_inc(S["wldV"].h, 16))
                    INC(S["wldV"], 16)

        def load_head_tabs(h):
            pb = h % 2
            P.emit("sy", lambda s, h=h, pb=pb: s.dma_start(
                out=dth_v(pb), in_=dtab[:, h * 2 * CH:(h + 1) * 2 * CH]).then_inc(S["tabh"].h, 16))
            INC(S["tabh"], 16)
            P.emit("sy", lambda s, h=h, pb=pb: s.dma_start(
                out=dqh_v(pb), in_=dqt[:, ts_(h, CH)]).then_inc(S["tabh"].h, 16))
            INC(S["tabh"], 16)
            P.emit("sy", lambda s, h=h, pb=pb: s.dma_start(
                out=dkh_v(pb), in_=dkt[:, ts_(h, CH)]).then_inc(S["tabh"].h, 16))
            INC(S["tabh"], 16)

        P.emit("ve", lambda v: v.memset(SF[:], 0.0).then_inc(S["ve"].h, 1))
        INC(S["ve"])
        P.wait("te", S["tab"], TABN)
        load_head_qk(0, 0)
        load_head_vg(0, 0)
        qk_done_te = {-1: 0}
        vg_done_te = {-1: 0}
        QDKD_last = 0
        rope_done_ve = 0
        xnt_war_te = {0: 0, 1: 0}
        ogw_prev = 0
        sbf_ac = 0
        dtab_used_ve = 0
        dtab_used_po = 0

        tab_war = {0: 0, 1: 0}
        for h in range(H):
            pb = h % 2
            P.wait("sy", S["ve"], tab_war[pb])
            load_head_tabs(h)
            tab_h = S["tabh"].n
            W_QK = (S["wldA"] if pb == 0 else S["wldB"]).n
            W_VG = S["wldV"].n
            if h + 1 < H:
                load_head_qk(h + 1, qk_done_te[h - 1])
            emit_cache_jobs(32)

            for til in range(NTIL):
                is_pre = til < NPRE
                tpar = til % 2
                XNT = xnt_v(tpar)
                P.wait("sy", S["xnb"], til + 1)
                P.wait("ac", S["xnb"], til + 1)
                P.wait("sy", S["te"], xnt_war_te[tpar])
                P.wait("ac", S["te"], xnt_war_te[tpar])
                xts = "xtA" if tpar == 0 else "xtB"
                xt2s = "xt2A" if tpar == 0 else "xt2B"
                for i_d in range(NDT):
                    sm = xts if i_d % 2 == 0 else xt2s
                    P.emit("sy", lambda s, til=til, i_d=i_d, XNT=XNT, sm=sm: s.dma_start_transpose(
                        XNT[:, ts_(i_d, TILE)],
                        xn_h[ts_(til, TILE), ts_(i_d, 128)]).then_inc(S[sm].h, 16))
                    INC(S[sm], 16)
                XT_N, XT2_N = S[xts].n, S[xt2s].n
                P.wait("te", S[xts], XT_N)
                P.wait("te", S[xt2s], XT2_N)
                P.wait("te", S["tabh"], tab_h)
                P.wait("te", S["ve"])
                P.wait("te", S["ac"])

                def xn_sl(i_d, tc, XNT=XNT):
                    return XNT[:, i_d * TILE + tc * 512: i_d * TILE + (tc + 1) * 512]

                P.wait("te", S["wldA"] if pb == 0 else S["wldB"], W_QK)
                tensors = [("k", wk_v(pb), KP)] if is_pre else \
                          [("q", wq_v(pb), QP), ("k", wk_v(pb), KP)]
                for nm, wsb, dstb in tensors:
                    P.wait("te", S["ve"])
                    def qk_mm(t, wsb=wsb, xn_sl=xn_sl):
                        mm = None
                        for dk_t in range(2):
                            for i_d in range(NDT):
                                wsl = wsb[:, i_d * 256 + dk_t * 128: i_d * 256 + (dk_t + 1) * 128]
                                for tc in range(2):
                                    mm = t.matmul(pf(dk_t * 2 + tc), wsl, xn_sl(i_d, tc),
                                                  start=(i_d == 0), stop=(i_d == NDT - 1))
                        mm.then_inc(S["te"].h, 1)
                    P.emit("te", qk_mm)
                    INC(S["te"])
                    MMQK = S["te"].n
                    P.wait("ve", S["te"], MMQK)
                    for tc in range(2):
                        P.wait("ac", S["ve"], rope_done_ve)
                        P.emit("ac", lambda s, til=til, tc=tc: s.dma_start(
                            out=COS[:], in_=rc_t[:, til * TILE + tc * 512: til * TILE + (tc + 1) * 512]).then_inc(S["rcs"].h, 16))
                        INC(S["rcs"], 16)
                        P.emit("ac", lambda s, til=til, tc=tc: s.dma_start(
                            out=SIN[:], in_=rs_t[:, til * TILE + tc * 512: til * TILE + (tc + 1) * 512]).then_inc(S["rcs"].h, 16))
                        INC(S["rcs"], 16)
                        ROPE_N = S["rcs"].n
                        P.wait("ve", S["rcs"], ROPE_N)
                        for half in range(2):
                            a_b, b_b = (0 + tc, 2 + tc) if half == 0 else (2 + tc, 0 + tc)
                            P.wait("ve", S["ve"])
                            P.emit("ve", lambda v, a_b=a_b, b_b=b_b: (
                                v.tensor_tensor(RT[:, 0:512], pf(a_b), COS[:], ALU.mult),
                                v.tensor_tensor(RT[:, 512:1024], pf(b_b), SIN[:], ALU.mult).then_inc(S["ve"].h, 1)))
                            INC(S["ve"])
                            P.wait("ve", S["ve"])
                            op = ALU.subtract if half == 0 else ALU.add
                            P.emit("ve", lambda v, tc=tc, dstb=dstb, half=half, op=op: v.tensor_tensor(
                                dstb[:, half * TILE + tc * 512: half * TILE + (tc + 1) * 512],
                                RT[:, 0:512], RT[:, 512:1024], op).then_inc(S["ve"].h, 1))
                            INC(S["ve"])
                            rope_done_ve = S["ve"].n
                qk_rope_ve = S["ve"].n
                if not is_pre:
                    qk_done_te[h] = S["te"].n

                P.wait("te", S["wldV"], W_VG)
                vdone_ve = {}
                gdone_ac = {}
                for tt8 in range(8):
                    vb = 4 + (tt8 % 2)
                    if tt8 >= 2:
                        P.wait("te", S["ve"], vdone_ve[tt8 - 2])
                    if tt8 >= 1 and not is_pre:
                        P.wait("te", S["ac"], gdone_ac[tt8 - 1])
                    def vg_mm(t, tt8=tt8, vb=vb, is_pre=is_pre, XNT=XNT):
                        mm = None
                        for i_d in range(NDT):
                            xsl = XNT[:, i_d * TILE + tt8 * 128: i_d * TILE + (tt8 + 1) * 128]
                            mm = t.matmul(pf(vb), xsl, WV_V[:, ts_(i_d, 512)],
                                          start=(i_d == 0), stop=(i_d == NDT - 1))
                            if not is_pre:
                                mm = t.matmul(pf(6), xsl, WG_V[:, ts_(i_d, 512)],
                                              start=(i_d == 0), stop=(i_d == NDT - 1))
                        mm.then_inc(S["te"].h, 1)
                    P.emit("te", vg_mm)
                    INC(S["te"])
                    MMV = S["te"].n
                    P.wait("ve", S["te"], MMV)
                    P.emit("ve", lambda v, tt8=tt8, vb=vb: v.tensor_copy(
                        VT[:, ts_(tt8, 512)], pf(vb)).then_inc(S["ve"].h, 1))
                    INC(S["ve"])
                    vdone_ve[tt8] = S["ve"].n
                    if not is_pre:
                        P.wait("ac", S["te"], MMV)
                        P.emit("ac", lambda a, tt8=tt8: a.activation(
                            GS[:, ts_(tt8, 512)], pf(6), AF.Silu).then_inc(S["ac"].h, 1))
                        INC(S["ac"])
                        gdone_ac[tt8] = S["ac"].n
                vg_mm_last = S["te"].n
                xnt_war_te[tpar] = vg_mm_last
                vg_done_te[h] = vg_mm_last
                V_DONE = vdone_ve[7]
                G_DONE = gdone_ac.get(7, 0)

                for c in range(NCH):
                    gch = til * NCH + c
                    par = gch % 2
                    P.wait("ve", S["tabh"], tab_h)
                    P.wait("ve", S["te"])
                    if not is_pre:
                        P.emit("ve", lambda v, c=c, pb=pb: (
                            v.tensor_tensor(QDC[:, 0:256], QP[:, c * CH:(c + 1) * CH], dqh_v(pb)[:], ALU.mult),
                            v.tensor_tensor(QDC[:, 256:512], QP[:, TILE + c * CH:TILE + (c + 1) * CH], dqh_v(pb)[:], ALU.mult).then_inc(S["ve"].h, 1)))
                        INC(S["ve"])
                    P.emit("ve", lambda v, c=c, pb=pb: (
                        v.tensor_tensor(KDC[:, 0:256], KP[:, c * CH:(c + 1) * CH], dkh_v(pb)[:], ALU.mult),
                        v.tensor_tensor(KDC[:, 256:512], KP[:, TILE + c * CH:TILE + (c + 1) * CH], dkh_v(pb)[:], ALU.mult).then_inc(S["ve"].h, 1)))
                    INC(S["ve"])
                    QDKD = S["ve"].n
                    QDKD_last = QDKD

                    P.wait("te", S["ve"], QDKD)
                    P.wait("te", S["ac"])
                    def kdt_mm(t):
                        mm = None
                        for j_t in range(2):
                            for dk_t in range(2):
                                mm = t.transpose(ps_b[:, (j_t * 2 + dk_t) * 128:(j_t * 2 + dk_t + 1) * 128],
                                                 KDC[:, dk_t * 256 + j_t * 128: dk_t * 256 + (j_t + 1) * 128],
                                                 id_s[:])
                        mm.then_inc(S["te"].h, 1)
                    P.emit("te", kdt_mm)
                    INC(S["te"])
                    P.wait("ac", S["te"], S["te"].n)
                    P.emit("ac", lambda a: a.copy(
                        kdT_s[:, 0:512], ps_b[:, 0:512]).then_inc(S["ac"].h, 1))
                    INC(S["ac"])
                    KDT_AC = S["ac"].n

                    O_MM = None
                    if not is_pre:
                        P.wait("te", S["ac"], G_DONE)
                        P.wait("te", S["ve"])
                        def sc_mm(t, c=c):
                            mm = None
                            for j_t in range(2):
                                for dk_t in range(2):
                                    mm = t.matmul(pf(6)[:, j_t * 256:(j_t + 1) * 256],
                                                  KP[:, dk_t * TILE + c * CH + j_t * 128: dk_t * TILE + c * CH + (j_t + 1) * 128],
                                                  QP[:, dk_t * TILE + c * CH: dk_t * TILE + (c + 1) * CH],
                                                  start=(dk_t == 0), stop=(dk_t == 1))
                            mm.then_inc(S["te"].h, 1)
                        P.emit("te", sc_mm)
                        INC(S["te"])
                        SC_MM = S["te"].n
                        P.wait("ve", S["te"], SC_MM)
                        P.emit("ve", lambda v, pb=pb: (
                            v.tensor_tensor(mt_s[:, 0:256], pf(6)[:, 0:256], dth_v(pb)[:, 0:256], ALU.mult),
                            v.tensor_tensor(mt_s[:, 256:512], pf(6)[:, 256:512], dth_v(pb)[:, 256:512], ALU.mult).then_inc(S["ve"].h, 1)))
                        INC(S["ve"])
                        MT_VE = S["ve"].n
                        dtab_used_ve = MT_VE

                    P.wait("te", S["ac"], KDT_AC)
                    P.wait("te", S["ve"], V_DONE)
                    P.wait("te", S["ve"])
                    def st_mm(t, c=c):
                        mm = None
                        for dk_t in range(2):
                            for j_t in range(2):
                                mm = t.matmul(pf(4 + dk_t),
                                              kdT_s[:, (j_t * 2 + dk_t) * 128: (j_t * 2 + dk_t + 1) * 128],
                                              VT[:, ts_(c * 2 + j_t, 512)],
                                              start=(j_t == 0), stop=(j_t == 1))
                        mm.then_inc(S["te"].h, 1)
                    P.emit("te", st_mm)
                    INC(S["te"])
                    ST_MM = S["te"].n

                    if not is_pre:
                        P.wait("te", S["ve"], MT_VE)
                        P.wait("te", S["ac"])
                        def o_mm(t, c=c, par=par):
                            sprev = sbf_v(1 - par)
                            mm = None
                            for i_t in range(2):
                                for j_t in range(2):
                                    t.matmul(pf(0 + i_t),
                                             mt_s[:, j_t * 256 + i_t * 128: j_t * 256 + (i_t + 1) * 128],
                                             VT[:, ts_(c * 2 + j_t, 512)],
                                             start=(j_t == 0), stop=False)
                                for dk_t in range(2):
                                    mm = t.matmul(pf(0 + i_t),
                                                  QDC[:, dk_t * 256 + i_t * 128: dk_t * 256 + (i_t + 1) * 128],
                                                  sprev[:, ts_(dk_t, 512)],
                                                  start=False, stop=(dk_t == 1))
                            mm.then_inc(S["te"].h, 1)
                        P.emit("te", o_mm)
                        INC(S["te"])
                        O_MM = S["te"].n

                    gc = float(GAM[h] ** CH)
                    P.wait("ve", S["te"], ST_MM)
                    P.wait("ve", S["ve"])
                    P.emit("ve", lambda v, gc=gc: (
                        v.scalar_tensor_tensor(out=SF[:, 0:512], in0=SF[:, 0:512], scalar=gc,
                                               in1=pf(4), op0=ALU.mult, op1=ALU.add),
                        v.scalar_tensor_tensor(out=SF[:, 512:1024], in0=SF[:, 512:1024], scalar=gc,
                                               in1=pf(5), op0=ALU.mult, op1=ALU.add).then_inc(S["ve"].h, 1)))
                    INC(S["ve"])
                    STT_VE = S["ve"].n
                    if not is_pre:
                        P.wait("ac", S["ve"], STT_VE)
                        P.wait("ac", S["te"], O_MM)
                        P.emit("ac", lambda a, par=par: a.copy(
                            sbf_v(par), SF[:]).then_inc(S["ac"].h, 1))
                        INC(S["ac"])
                        sbf_ac = S["ac"].n

                        P.wait("ve", S["te"], O_MM)
                        P.wait("ve", S["ac"], G_DONE)
                        if c == 0:
                            P.wait("ve", S["ogw"], ogw_prev)
                        P.emit("ve", lambda v, c=c: (
                            v.tensor_tensor(OGB[:, ts_(c * 2, 512)], pf(0), GS[:, ts_(c * 2, 512)], ALU.mult),
                            v.tensor_tensor(OGB[:, ts_(c * 2 + 1, 512)], pf(1), GS[:, ts_(c * 2 + 1, 512)], ALU.mult).then_inc(S["ve"].h, 1)))
                        INC(S["ve"])
                        OS_VE = S["ve"].n
                        P.wait("ac", S["ve"], OS_VE)
                        P.emit("ac", lambda a, c=c: (
                            a.activation(pf(0), pf(0), AF.Square,
                                         accum_out=ms_s[:, c * 2:c * 2 + 1]),
                            a.activation(pf(1), pf(1), AF.Square,
                                         accum_out=ms_s[:, c * 2 + 1:c * 2 + 2]).then_inc(S["ac"].h, 1)))
                        INC(S["ac"])

                if is_pre and til == NPRE - 1:
                    P.wait("ve", S["tab"], TABN)
                    P.wait("ve", S["ve"])
                    P.emit("ve", lambda v: (
                        v.tensor_scalar(out=SF[:, 0:512], in0=SF[:, 0:512],
                                        scalar1=mask_s[:, 0:1], scalar2=None, op0=ALU.mult),
                        v.tensor_scalar(out=SF[:, 512:1024], in0=SF[:, 512:1024],
                                        scalar1=mask_s[:, 0:1], scalar2=None, op0=ALU.mult).then_inc(S["ve"].h, 1)))
                    INC(S["ve"])
                    P.wait("ac", S["ve"])
                    P.emit("ac", lambda a: a.copy(sbf_v(1), SF[:]).then_inc(S["ac"].h, 1))
                    INC(S["ac"])
                    sbf_ac = S["ac"].n

                if not is_pre:
                    AC_MS = S["ac"].n
                    P.wait("ac", S["ac"], AC_MS)
                    P.wait("ac", S["ve"])
                    P.emit("ac", lambda a: a.activation(
                        rstd_s[:], ms_s[:], AF.Sqrt, bias=epsc[:, 0:1],
                        scale=float(1.0 / DV)).then_inc(S["ac"].h, 1))
                    INC(S["ac"])
                    P.wait("ve", S["ac"])
                    P.emit("ve", lambda v: v.reciprocal(rstd_s[:], rstd_s[:]).then_inc(S["ve"].h, 1))
                    INC(S["ve"])
                    P.wait("ve", S["ve"])
                    def og_scale(v):
                        mm = None
                        for sub in range(8):
                            mm = v.tensor_scalar(out=OGB[:, ts_(sub, 512)], in0=OGB[:, ts_(sub, 512)],
                                                 scalar1=rstd_s[:, sub:sub + 1], scalar2=None,
                                                 op0=ALU.mult)
                        mm.then_inc(S["ve"].h, 1)
                    P.emit("ve", og_scale)
                    INC(S["ve"])
                    OG_VE = S["ve"].n
                    P.wait("po", S["ve"], OG_VE)
                    t0 = (til - NPRE) * TILE
                    P.emit("po", lambda s, h=h, t0=t0: s.dma_start(
                        out=og_h[h, t0:t0 + TILE, :].rearrange("(a p) c -> p a c", p=128),
                        in_=OGB.rearrange("p (a c) -> p a c", c=512)).then_inc(S["ogw"].h, 16))
                    INC(S["ogw"], 16)
                    ogw_prev = S["ogw"].n
            if h + 1 < H:
                load_head_vg(h + 1, vg_done_te[h])
            P.wait("ve", S["ve"])
            P.emit("ve", lambda v: v.memset(SF[:], 0.0).then_inc(S["ve"].h, 1))
            INC(S["ve"])
            tab_war[pb] = max(dtab_used_ve, QDKD_last)
        emit_cache_jobs(len(cache_jobs))
        CACHE_DONE = S["wcw"].n
        P1_END = dict(te=S["te"].n, ve=S["ve"].n, ac=S["ac"].n, po=S["po"].n,
                      ogw=S["ogw"].n)

        for e in ("sy", "ac", "po", "te", "ve"):
            P.wait(e, S["te"], P1_END["te"])
            P.wait(e, S["ve"], P1_END["ve"])
            P.wait(e, S["ac"], P1_END["ac"])
            P.wait(e, S["po"], P1_END["po"])
        P.wait("sy", S["ogw"], P1_END["ogw"])
        P.wait("ac", S["ogw"], P1_END["ogw"])
        P.wait("sy", S["wcw"], CACHE_DONE)
        P.wait("ac", S["wcw"], CACHE_DONE)

        fst2_last = {i: None for i in range(4)}
        wo_done = {}
        ogt_sems = (P.sem("ogt"), P.sem("ogtB"))
        for ep in range(2):
            if ep > 0:
                P.wait("sy", S["te"])
                P.wait("ac", S["te"])
                P.wait("sy", S["po"])
                P.wait("ac", S["po"])
            for hv_t in range(32):
                qe = "ac" if hv_t % 2 == 0 else "po"
                P.emit(qe, lambda s, hv_t=hv_t, ep=ep: s.dma_start(
                    out=WOH[:, ts_(hv_t, 1024)],
                    in_=wo_bf[ts_(hv_t, 128), ts_(ep, 1024)]).then_inc(S["wo2"].h, 16))
                INC(S["wo2"], 16)
            WO_LD = S["wo2"].n
            P.wait("po", S["wo2"], WO_LD)
            P.wait("po", S["tab"], TABN)
            def gnw_scale(g):
                mm = None
                for hv_t in range(32):
                    mm = g.tensor_scalar(out=WOH[:, ts_(hv_t, 1024)], in0=WOH[:, ts_(hv_t, 1024)],
                                         scalar1=gnw_s[:, hv_t % 4:hv_t % 4 + 1], scalar2=None,
                                         op0=ALU.mult)
                mm.then_inc(S["po"].h, 1)
            P.emit("po", gnw_scale)
            INC(S["po"])
            WO_SC = S["po"].n

            for tg in range(8):
                par = tg % 2
                osem = ogt_sems[par]
                ogv = OGT[:, par * 16384:(par + 1) * 16384]
                P.wait("sy", S["te"], wo_done.get((ep, tg - 2), 0))
                if ep > 0 and tg < 2:
                    P.wait("sy", S["te"], wo_done.get((ep - 1, 6 + tg), 0))
                for hv_t in range(32):
                    hh, dv0 = hv_t // 4, (hv_t % 4) * 128
                    P.emit("sy", lambda s, hh=hh, dv0=dv0, tg=tg, hv_t=hv_t, ogv=ogv, osem=osem: s.dma_start_transpose(
                        ogv[:, ts_(hv_t, 512)],
                        og_h[hh, ts_(tg, 512), dv0:dv0 + 128]).then_inc(osem.h, 16))
                    INC(osem, 16)
                OGT_LD = osem.n
                P.wait("te", osem, OGT_LD)
                P.wait("te", S["po"], WO_SC)
                evac_ve = {}
                for et in range(8):
                    bk = et % 4
                    if et >= 4:
                        P.wait("te", S["ve"], evac_ve[et - 4])
                    def wo_mm(t, et=et, bk=bk, ogv=ogv):
                        mm = None
                        for hv_t in range(32):
                            wsl = WOH[:, hv_t * 1024 + et * 128: hv_t * 1024 + (et + 1) * 128]
                            mm = t.matmul(pf(bk), wsl, ogv[:, ts_(hv_t, 512)],
                                          start=(hv_t == 0), stop=(hv_t == 31))
                        mm.then_inc(S["te"].h, 1)
                    P.emit("te", wo_mm)
                    INC(S["te"])
                    WOMM = S["te"].n
                    slot = et % 4
                    aw = S["atwA"] if et % 2 == 0 else S["atwB"]
                    P.wait("ve", S["te"], WOMM)
                    if fst2_last[slot] is not None:
                        P.wait("ve", fst2_last[slot][0], fst2_last[slot][1])
                    P.emit("ve", lambda v, bk=bk, slot=slot: v.tensor_copy(
                        FST2[:, slot * 512:(slot + 1) * 512], pf(bk)).then_inc(S["ve"].h, 1))
                    INC(S["ve"])
                    evac_ve[et] = S["ve"].n
                    P.wait("po", S["ve"], S["ve"].n)
                    P.emit("po", lambda s, ep=ep, et=et, tg=tg, slot=slot, aw=aw: s.dma_start(
                        out=at_h[ep * 1024 + et * 128: ep * 1024 + (et + 1) * 128, ts_(tg, 512)],
                        in_=FST2[:, slot * 512:(slot + 1) * 512]).then_inc(aw.h, 16))
                    INC(aw, 16)
                    fst2_last[slot] = (aw, aw.n)
                wo_done[(ep, tg)] = S["te"].n
        P2_END = dict(te=S["te"].n, ve=S["ve"].n, po=S["po"].n,
                      atwA=S["atwA"].n, atwB=S["atwB"].n)

        for e in ("sy", "ac", "po", "te", "ve"):
            P.wait(e, S["te"], P2_END["te"])
            P.wait(e, S["ve"], P2_END["ve"])
        P.wait("sy", S["atwA"], P2_END["atwA"])
        P.wait("ac", S["atwA"], P2_END["atwA"])
        P.wait("sy", S["atwB"], P2_END["atwB"])
        P.wait("ac", S["atwB"], P2_END["atwB"])

        fw_war_te = {0: 0, 1: 0}
        xnf_war_te = {0: 0, 1: 0}
        outw_after = {}
        po_add_after = {}
        for ft in range(NFT):
            t0 = TP + ft * TILE
            for tc in range(2):
                P.wait("sy", S["te"], xnf_war_te[tc])
                P.wait("ac", S["te"], xnf_war_te[tc])
                xts = "xtA" if tc == 0 else "xtB"
                xt2s = "xt2A" if tc == 0 else "xt2B"
                for i_d in range(NDT):
                    sm = xts if i_d % 2 == 0 else xt2s
                    P.emit("sy", lambda s, i_d=i_d, t0=t0, tc=tc, sm=sm: s.dma_start_transpose(
                        xnf_v(tc)[:, ts_(i_d, 512)],
                        xn_h[t0 + tc * 512: t0 + (tc + 1) * 512, ts_(i_d, 128)]).then_inc(S[sm].h, 16))
                    INC(S[sm], 16)
            XNF_N = (S["xtA"].n, S["xtB"].n, S["xt2A"].n, S["xt2B"].n)

            hid_ve = {}
            silu_ac = {}
            for h_t in range(HT):
                pbw = h_t % 2
                fwsem = S["fwA"] if pbw == 0 else S["fwB"]
                qe = "sy" if h_t % 2 == 0 else "ac"
                P.wait(qe, S["te"], fw_war_te[pbw])
                P.emit(qe, lambda s, h_t=h_t, pbw=pbw, fwsem=fwsem: s.dma_start(
                    out=wdb_v(pbw), in_=wgu_bl[h_t]).then_inc(fwsem.h, 16))
                INC(fwsem, 16)
                FW_LD = fwsem.n
                P.wait("te", fwsem, FW_LD)
                P.wait("te", S["xtA"], XNF_N[0])
                P.wait("te", S["xtB"], XNF_N[1])
                P.wait("te", S["xt2A"], XNF_N[2])
                P.wait("te", S["xt2B"], XNF_N[3])
                gb = 0 if h_t % 2 == 0 else 4
                ub = (2, 3) if h_t % 2 == 0 else (6, 3)
                if h_t >= 2:
                    P.wait("te", S["ac"], silu_ac.get(h_t - 2, 0))
                def f1_gate(t, pbw=pbw, gb=gb):
                    mm = None
                    wb = wdb_v(pbw)
                    for tc in range(2):
                        for i_d in range(NDT):
                            mm = t.matmul(pf(gb + tc), wb[:, i_d * 256:i_d * 256 + 128],
                                          xnf_v(tc)[:, ts_(i_d, 512)],
                                          start=(i_d == 0), stop=(i_d == NDT - 1))
                        mm.then_inc(S["te"].h, 1)
                P.emit("te", f1_gate)
                INC(S["te"], 2)
                G_TC = (S["te"].n - 1, S["te"].n)
                if h_t >= 2:
                    P.wait("te", S["ve"], hid_ve[(h_t - 2, 0)])
                if h_t >= 1:
                    P.wait("te", S["ve"], hid_ve[(h_t - 1, 1)])
                def f1_up(t, pbw=pbw, ub=ub):
                    mm = None
                    wb = wdb_v(pbw)
                    for tc in range(2):
                        for i_d in range(NDT):
                            mm = t.matmul(pf(ub[tc]), wb[:, i_d * 256 + 128:i_d * 256 + 256],
                                          xnf_v(tc)[:, ts_(i_d, 512)],
                                          start=(i_d == 0), stop=(i_d == NDT - 1))
                        mm.then_inc(S["te"].h, 1)
                P.emit("te", f1_up)
                INC(S["te"], 2)
                U_TC = (S["te"].n - 1, S["te"].n)
                F1MM = S["te"].n
                fw_war_te[pbw] = F1MM
                for tc in range(2):
                    P.wait("ac", S["te"], G_TC[tc])
                    P.wait("ac", S["ve"],
                           hid_ve[(h_t - 1, 1)] if (tc == 0 and h_t >= 1)
                           else (hid_ve[(h_t, 0)] if tc == 1 else 0))
                    P.emit("ac", lambda a, tc=tc, gb=gb: a.activation(
                        sgq[:], pf(gb + tc), AF.Silu).then_inc(S["ac"].h, 1))
                    INC(S["ac"])
                    if tc == 1:
                        silu_ac[h_t] = S["ac"].n
                    P.wait("ve", S["ac"], S["ac"].n)
                    P.wait("ve", S["te"], U_TC[tc])
                    P.emit("ve", lambda v, h_t=h_t, tc=tc, ub=ub: v.tensor_tensor(
                        HIDB[:, h_t * 1024 + tc * 512: h_t * 1024 + (tc + 1) * 512],
                        sgq[:], pf(ub[tc]), ALU.mult).then_inc(S["ve"].h, 1))
                    INC(S["ve"])
                    hid_ve[(h_t, tc)] = S["ve"].n
            HID_VE = S["ve"].n
            for tc in range(2):
                xnf_war_te[tc] = S["te"].n
            if debug and ft == 0:
                P.wait("sy", S["ve"], HID_VE)
                P.emit("sy", lambda s: s.dma_start(out=hid_dbg, in_=HIDB).then_inc(S["fr"].h, 16))
                INC(S["fr"], 16)

            P.wait("te", S["ve"], HID_VE)
            P.wait("te", S["ac"], silu_ac.get(HT - 1, 0))
            dt_evac_ve = {}
            for d_t in range(NDT):
                bk = (d_t % 2) * 2
                fw_lds = []
                for hf in range(2):
                    slot = (d_t * 2 + hf) % 2
                    fwsem = S["fwA"] if slot == 0 else S["fwB"]
                    qe = "sy" if hf == 0 else "ac"
                    P.wait(qe, S["te"], fw_war_te[slot])
                    P.emit(qe, lambda s, d_t=d_t, hf=hf, slot=slot, fwsem=fwsem: s.dma_start(
                        out=wdb_v(slot),
                        in_=wd_bl[d_t, :, hf * 4096:(hf + 1) * 4096]).then_inc(fwsem.h, 16))
                    INC(fwsem, 16)
                    fw_lds.append((fwsem, fwsem.n))
                qe = "ac"
                if d_t >= 1:
                    P.wait(qe, S["ve"], dt_evac_ve[d_t - 1])
                    P.wait(qe, S["po"], po_add_after[d_t - 1])
                P.emit(qe, lambda s, d_t=d_t, ft=ft: s.dma_start(
                    out=FST4[:, 0:1024], in_=at_h[ts_(d_t, 128), ts_(ft, TILE)]).then_inc(S["fr"].h, 16))
                INC(S["fr"], 16)
                P.emit(qe, lambda s, d_t=d_t, ft=ft: s.dma_start(
                    out=FST4[:, 1024:2048], in_=x_fm[ts_(d_t, 128), ts_(ft, TILE)]).then_inc(S["fr"].h, 16))
                INC(S["fr"], 16)
                FR_LD = S["fr"].n
                if d_t >= 2:
                    P.wait("te", S["ve"], dt_evac_ve[d_t - 2])
                for hf in range(2):
                    slot = (d_t * 2 + hf) % 2
                    P.wait("te", fw_lds[hf][0], fw_lds[hf][1])
                    def f2_mm(t, hf=hf, slot=slot, bk=bk):
                        wb = wdb_v(slot)
                        mm = None
                        for hk in range(32):
                            wsl = wb[:, ts_(hk, 128)]
                            hkg = hf * 32 + hk
                            for tc in range(2):
                                mm = t.matmul(pf(bk + tc), wsl,
                                              HIDB[:, hkg * 1024 + tc * 512: hkg * 1024 + (tc + 1) * 512],
                                              start=(hkg == 0), stop=(hkg == HT - 1))
                        mm.then_inc(S["te"].h, 1)
                    P.emit("te", f2_mm)
                    INC(S["te"])
                    fw_war_te[slot] = S["te"].n
                F2MM = S["te"].n
                P.wait("ve", S["te"], F2MM)
                P.wait("ve", S["fr"], FR_LD)
                if d_t >= 1:
                    P.wait("ve", S["po"], po_add_after[d_t - 1])
                P.emit("ve", lambda v, bk=bk: (
                    v.tensor_tensor(RT4[:, 0:512], pf(bk), FST4[:, 0:512], ALU.add),
                    v.tensor_tensor(RT4[:, 512:1024], pf(bk + 1), FST4[:, 512:1024], ALU.add).then_inc(S["ve"].h, 1)))
                INC(S["ve"])
                dt_evac_ve[d_t] = S["ve"].n
                P.wait("po", S["ve"], S["ve"].n)
                key = (ft, d_t - 1)
                if key in outw_after:
                    P.wait("po", S["outw"], outw_after[key])
                elif ft > 0:
                    P.wait("po", S["outw"], outw_after[(ft - 1, NDT - 1)])
                P.emit("po", lambda g: (
                    g.tensor_tensor(RT4[:, 1024:1536], RT4[:, 0:512], FST4[:, 1024:1536], ALU.add),
                    g.tensor_tensor(RT4[:, 1536:2048], RT4[:, 512:1024], FST4[:, 1536:2048], ALU.add).then_inc(S["po"].h, 1)))
                INC(S["po"])
                po_add_after[d_t] = S["po"].n
                P.emit("po", lambda g, d_t=d_t, ft=ft: g.dma_start(
                    out=out_fm[ts_(d_t, 128), ts_(ft, TILE)],
                    in_=RT4[:, 1024:2048]).then_inc(S["outw"].h, 16))
                INC(S["outw"], 16)
                outw_after[(ft, d_t)] = S["outw"].n

        P.wait("sy", S["outw"], S["outw"].n)

        for name in P.sems:
            P.sems[name].h = E(nc.semaphore(name))
        blk = E(nc.Block())

        @blk.tensor
        def _(t):
            for fn in P.q["te"]:
                fn(t)

        @blk.vector
        def _(v):
            for fn in P.q["ve"]:
                fn(v)

        @blk.scalar
        def _(a):
            for fn in P.q["ac"]:
                fn(a)

        @blk.gpsimd
        def _(g):
            for fn in P.q["po"]:
                fn(g)

        @blk.sync
        def _(s):
            for fn in P.q["sy"]:
                fn(s)

    return nc


def _host_constants():
    j = np.arange(128, dtype=np.float64)
    inv_freq = 1.0 / (10000.0 ** (2.0 * j / DK))
    consts = {}
    for half in range(2):
        pos_main = np.arange(half * TM, half * TM + TM, dtype=np.float64)
        if half == 1:
            pos = np.concatenate([np.arange(0, TP, dtype=np.float64), pos_main])
        else:
            pos = np.concatenate([pos_main, pos_main])
        ang = pos[None, :] * inv_freq[:, None]
        consts[(half, "cos")] = np.ascontiguousarray(np.cos(ang).astype(np.float32))
        consts[(half, "sin")] = np.ascontiguousarray(np.sin(ang).astype(np.float32))
    ii = np.arange(CH, dtype=np.float64)
    dtab_ = np.zeros((128, H * 2 * CH), dtype=np.float32)
    dq = np.zeros((128, H * CH), dtype=np.float32)
    dk = np.zeros((128, H * CH), dtype=np.float32)
    for h in range(H):
        g = GAM[h]
        for j_t in range(2):
            jj = np.arange(128, dtype=np.float64)[:, None] + j_t * 128
            diff = ii[None, :] - jj
            m = (diff >= 0).astype(np.float64)
            dtab_[:, (h * 2 + j_t) * CH:(h * 2 + j_t + 1) * CH] = \
                (np.power(g, np.maximum(diff, 0.0)) * m * QSC).astype(np.float32)
        dq[:, h * CH:(h + 1) * CH] = np.broadcast_to(
            (np.power(g, ii + 1.0) * QSC).astype(np.float32), (128, CH))
        dk[:, h * CH:(h + 1) * CH] = np.broadcast_to(
            np.power(g, CH - 1.0 - ii).astype(np.float32), (128, CH))
    consts["dtab"] = dtab_.astype(ml_dtypes.bfloat16)
    consts["dq_bf"] = dq.astype(ml_dtypes.bfloat16)
    consts["dk_bf"] = dk.astype(ml_dtypes.bfloat16)
    consts["ident"] = np.eye(128, dtype=np.float32).astype(ml_dtypes.bfloat16)
    return consts


_NC_CACHE = {}

def _get_nc():
    if "nc" not in _NC_CACHE:
        _NC_CACHE["nc"] = build_program()
    return _NC_CACHE["nc"]


def kernel(L0_attn_hidden_states, L0_attn_Wq, L0_attn_Wk, L0_attn_Wv,
           L0_attn_Wg, L0_attn_Wo, L0_attn_g_norm_weight,
           L0_ffn_w_gate, L0_ffn_w_up, L0_ffn_w_down):
    x = np.asarray(L0_attn_hidden_states, dtype=np.float32)
    consts = _host_constants()
    gn = np.asarray(L0_attn_g_norm_weight, np.float32)
    gnwc = np.ascontiguousarray(gn.reshape(DV // 128, 128).T)

    wgt_a = np.ascontiguousarray(np.asarray(L0_ffn_w_gate, np.float32).T)
    wut_a = np.ascontiguousarray(np.asarray(L0_ffn_w_up, np.float32).T)
    wdt_a = np.ascontiguousarray(np.asarray(L0_ffn_w_down, np.float32).T)

    shared = {
        "wq": np.ascontiguousarray(np.asarray(L0_attn_Wq, np.float32)),
        "wk": np.ascontiguousarray(np.asarray(L0_attn_Wk, np.float32)),
        "wv": np.ascontiguousarray(np.asarray(L0_attn_Wv, np.float32)),
        "wg": np.ascontiguousarray(np.asarray(L0_attn_Wg, np.float32)),
        "wo": np.ascontiguousarray(np.asarray(L0_attn_Wo, np.float32)),
        "wgt": wgt_a, "wut": wut_a, "wdt": wdt_a,
        "dtab": consts["dtab"], "dqt": consts["dq_bf"], "dkt": consts["dk_bf"],
        "gnwc": gnwc, "identt": consts["ident"],
    }
    in_maps = []
    for c in range(8):
        b, half = c // 2, c % 2
        xb = x[b]
        main = xb[half * TM:(half + 1) * TM]
        pre = xb[0:TP] if half == 1 else main
        m = dict(shared)
        m["x_tm"] = np.ascontiguousarray(np.concatenate([pre, main], axis=0))
        m["x_fm"] = np.ascontiguousarray(main.T)
        m["maskc"] = np.full((128, 1), 1.0 if half == 1 else 0.0, np.float32)
        m["ropec"] = consts[(half, "cos")]
        m["ropes"] = consts[(half, "sin")]
        in_maps.append(m)
    nc = _get_nc()
    res = run_bass_kernel_spmd(nc, in_maps, core_ids=list(range(8)))
    out = np.empty((B, T, D), dtype=np.float32)
    for c in range(8):
        b, half = c // 2, c % 2
        out[b, half * TM:(half + 1) * TM, :] = res.results[c]["out_fm"].T
    return out


# version 12
# speedup vs baseline: 1.0539x; 1.0539x over previous
import numpy as np
import ml_dtypes
import concourse.bass as bass
import concourse.mybir as mybir
from concourse.bass_utils import run_bass_kernel_spmd
from contextlib import ExitStack

F32 = mybir.dt.float32
BF16 = mybir.dt.bfloat16
AF = mybir.ActivationFunctionType
ALU = mybir.AluOpType

B, T, D, H, DK, DV, HID = 4, 8192, 2048, 8, 256, 512, 8192
TP, TM = 4096, 4096
TS = TP + TM
TILE, CH = 1024, 256
NDT = D // 128
NTIL = TS // TILE
NPRE = TP // TILE
NCH = TILE // CH
NFT = TM // TILE
HT = HID // 128
EPS = 1e-5
GAM = [1.0 - 2.0 ** (-5.0 - h) for h in range(H)]
QSC = DK ** -0.5


class Sem:
    __slots__ = ("name", "h", "n")
    def __init__(self, name):
        self.name, self.h, self.n = name, None, 0


class Plan:
    ENGS = ("te", "ve", "ac", "po", "sy")
    def __init__(self):
        self.q = {e: [] for e in self.ENGS}
        self.waited = {e: {} for e in self.ENGS}
        self.sems = {}
        self.paranoid = False
        self.selfpar = ()
        self.cpar = ()

    def sem(self, name):
        if name not in self.sems:
            self.sems[name] = Sem(name)
        return self.sems[name]

    def emit(self, eng, fn):
        if self.paranoid and (self.paranoid is True or eng in self.paranoid):
            for s in self.sems.values():
                self.wait(eng, s)
        elif self.cpar and eng in self.cpar:
            for sn in ("te", "ve", "ac", "po"):
                self.wait(eng, self.sems[sn])
        elif self.selfpar and eng in self.selfpar and eng in ("ve", "po", "ac", "te"):
            self.wait(eng, self.sems[eng])
        self.q[eng].append(fn)

    def wait(self, eng, sem, cnt=None):
        if cnt is None:
            cnt = sem.n
        if cnt <= 0:
            return
        w = self.waited[eng]
        if w.get(sem.name, -1) >= cnt:
            return
        w[sem.name] = cnt
        self.q[eng].append(lambda e, s=sem, c=cnt: e.wait_ge(s.h, c))


def ts_(i, n):
    return slice(i * n, (i + 1) * n)


def build_program(debug=False, paranoid=False, selfpar=(), cpar=()):
    nc = bass.Bass("TRN2", target_bir_lowering=False)
    P = Plan()
    P.paranoid = paranoid
    P.selfpar = selfpar
    P.cpar = cpar
    dbg_kind = "ExternalOutput" if debug else "Internal"

    x_tm = nc.dram_tensor("x_tm", [TS, D], F32, kind="ExternalInput").ap()
    x_fm = nc.dram_tensor("x_fm", [D, TM], F32, kind="ExternalInput").ap()
    wq = nc.dram_tensor("wq", [D, H * DK], F32, kind="ExternalInput").ap()
    wk = nc.dram_tensor("wk", [D, H * DK], F32, kind="ExternalInput").ap()
    wv = nc.dram_tensor("wv", [D, H * DV], F32, kind="ExternalInput").ap()
    wg = nc.dram_tensor("wg", [D, H * DV], F32, kind="ExternalInput").ap()
    wo = nc.dram_tensor("wo", [H * DV, D], F32, kind="ExternalInput").ap()
    wgt = nc.dram_tensor("wgt", [D, HID], F32, kind="ExternalInput").ap()
    wut = nc.dram_tensor("wut", [D, HID], F32, kind="ExternalInput").ap()
    wdt = nc.dram_tensor("wdt", [HID, D], F32, kind="ExternalInput").ap()
    rc_t = nc.dram_tensor("ropec", [128, TS], F32, kind="ExternalInput").ap()
    rs_t = nc.dram_tensor("ropes", [128, TS], F32, kind="ExternalInput").ap()
    dtab = nc.dram_tensor("dtab", [128, H * 2 * CH], BF16, kind="ExternalInput").ap()
    dqt = nc.dram_tensor("dqt", [128, H * CH], BF16, kind="ExternalInput").ap()
    dkt = nc.dram_tensor("dkt", [128, H * CH], BF16, kind="ExternalInput").ap()
    gnwc = nc.dram_tensor("gnwc", [128, DV // 128], F32, kind="ExternalInput").ap()
    maskc = nc.dram_tensor("maskc", [128, 1], F32, kind="ExternalInput").ap()
    identt = nc.dram_tensor("identt", [128, 128], BF16, kind="ExternalInput").ap()

    out_fm = nc.dram_tensor("out_fm", [D, TM], F32, kind="ExternalOutput").ap()

    xn_h = nc.dram_tensor("xn_h", [TS, D], BF16, kind=dbg_kind).ap()
    og_h = nc.dram_tensor("og_h", [H, TM, DV], BF16, kind=dbg_kind).ap()
    at_h = nc.dram_tensor("at_h", [D, TM], F32, kind=dbg_kind).ap()
    wo_bf = nc.dram_tensor("wo_bf", [H * DV, D], BF16, kind=dbg_kind).ap()
    wgu_bl = nc.dram_tensor("wgu_bl", [HT, 128, 4096], BF16, kind=dbg_kind).ap()
    wd_bl = nc.dram_tensor("wd_bl", [NDT, 128, HT * 128], BF16, kind=dbg_kind).ap()
    hid_dbg = (nc.dram_tensor("hid_dbg", [128, 65536], BF16, kind="ExternalOutput").ap()
               if debug else None)

    with ExitStack() as ctx:
        E = ctx.enter_context

        ARB = E(nc.sbuf_tensor("arb", [128, 92160], BF16))
        ARF = E(nc.sbuf_tensor("arf", [128, 5632], F32))
        XNS = ARB[:, 90112:92160]
        CSTG = ARB[:, 90112:92160]
        def wq_v(pb): return ARB[:, pb * 4096:(pb + 1) * 4096]
        def wk_v(pb): return ARB[:, 8192 + pb * 4096:8192 + (pb + 1) * 4096]
        WV_V = ARB[:, 16384:24576]
        WG_V = ARB[:, 24576:32768]
        def xnt_v(pb): return ARB[:, 32768 + pb * 16384:32768 + (pb + 1) * 16384]
        OGB = ARB[:, 65536:69632]
        VT = ARB[:, 69632:73728]
        GS = ARB[:, 73728:77824]
        QP = ARB[:, 77824:79872]
        KP = ARB[:, 79872:81920]
        QDC = ARB[:, 81920:82432]
        KDC = ARB[:, 82432:82944]
        def sbf_v(pb): return ARB[:, 82944 + pb * 1024:82944 + (pb + 1) * 1024]
        def dth_v(pb): return ARB[:, 85504 + pb * 512:85504 + (pb + 1) * 512]
        def dqh_v(pb): return ARB[:, 86528 + pb * 256:86528 + (pb + 1) * 256]
        def dkh_v(pb): return ARB[:, 87040 + pb * 256:87040 + (pb + 1) * 256]
        OGT = ARB[:, 0:32768]
        WOH = ARB[:, 32768:65536]
        HIDB = ARB[:, 0:65536]
        def xnf_v(tc): return ARB[:, 65536 + tc * 8192:65536 + (tc + 1) * 8192]
        def wdb_v(pb): return ARB[:, 81920 + pb * 4096:81920 + (pb + 1) * 4096]

        XS = ARF[:, 0:2048]
        RT = ARF[:, 2048:3072]
        COS = ARF[:, 3072:3584]
        SIN = ARF[:, 3584:4096]
        SF = ARF[:, 4608:5632]
        FST2 = ARF[:, 0:2048]
        RT4 = ARF[:, 2048:4096]
        FST4 = ARF[:, 0:2048]

        bn6 = E(nc.sbuf_tensor("bn6", [128, 24], F32))
        mcol = E(nc.sbuf_tensor("mcol", [128, 2], F32))
        epsc = E(nc.sbuf_tensor("epsc", [128, 1], F32))
        ms_s = E(nc.sbuf_tensor("ms_s", [128, 2 * NCH], F32))
        rstd_s = E(nc.sbuf_tensor("rstd_s", [128, 2 * NCH], F32))
        gnw_s = E(nc.sbuf_tensor("gnw_s", [128, DV // 128], F32))
        mask_s = E(nc.sbuf_tensor("mask_s", [128, 1], F32))
        id_s = E(nc.sbuf_tensor("id_s", [128, 128], BF16))
        sgq = E(nc.sbuf_tensor("sgq", [128, 512], BF16))
        mt_s = E(nc.sbuf_tensor("mt_s", [128, 512], BF16))
        kdT_s = E(nc.sbuf_tensor("kdT_s", [128, 512], BF16))
        ps_f = E(nc.psum_tensor("ps_f", [128, 7 * 512], F32))
        ps_b = E(nc.psum_tensor("ps_b", [128, 1024], BF16))

        def pf(bank, w=512):
            return ps_f[:, bank * 512:bank * 512 + w]

        for s in ("xwA", "xwB", "xnw", "xnb", "xtA", "xtB", "xt2A", "xt2B",
                  "wldA", "wldB", "wldV", "cld", "wcw", "ogw", "ogt", "wo2",
                  "fwA", "fwB", "tab", "tabh", "rcs", "atwA", "atwB", "fr",
                  "outw", "te", "ve", "ac", "po"):
            P.sem(s)
        S = P.sems

        def INC(sem, k=1):
            sem.n += k
            return sem.n

        def _tabs(s):
            s.dma_start(out=gnw_s[:], in_=gnwc).then_inc(S["tab"].h, 16)
            s.dma_start(out=mask_s[:], in_=maskc).then_inc(S["tab"].h, 16)
            s.dma_start(out=id_s[:], in_=identt).then_inc(S["tab"].h, 16)
        P.emit("sy", _tabs)
        INC(S["tab"], 48)
        TABN = S["tab"].n

        P.emit("ve", lambda v: v.memset(epsc[:], float(EPS)).then_inc(S["ve"].h, 1))
        INC(S["ve"])
        EPS_VE = S["ve"].n
        NS = TS // 128
        ve_xn = {}
        xnw_cnt = {}
        for i in range(NS):
            xw = S["xwA"] if i % 2 == 0 else S["xwB"]
            if i >= 1:
                P.wait("sy", S["ve"], ve_xn[i - 1])
            P.emit("sy", lambda s, i=i, xw=xw: s.dma_start(
                out=XS[:], in_=x_tm[ts_(i, 128), :]).then_inc(xw.h, 16))
            INC(xw, 16)
            P.wait("ve", xw)
            P.wait("ve", S["ve"])
            P.emit("ve", lambda v: (
                v.bn_stats(bn6[:, 0:6], XS[:, 0:512]),
                v.bn_stats(bn6[:, 6:12], XS[:, 512:1024]),
                v.bn_stats(bn6[:, 12:18], XS[:, 1024:1536]),
                v.bn_stats(bn6[:, 18:24], XS[:, 1536:2048]).then_inc(S["ve"].h, 1)))
            INC(S["ve"])
            P.wait("ve", S["ve"])
            P.emit("ve", lambda v: v.bn_aggr(mcol[:], bn6[:]).then_inc(S["ve"].h, 1))
            INC(S["ve"])
            P.wait("ac", S["ve"])
            P.emit("ac", lambda a: a.activation(
                mcol[:, 1:2], mcol[:, 1:2], AF.Sqrt, bias=epsc[:, 0:1]).then_inc(S["ac"].h, 1))
            INC(S["ac"])
            P.wait("ve", S["ac"])
            P.emit("ve", lambda v: v.reciprocal(mcol[:, 1:2], mcol[:, 1:2]).then_inc(S["ve"].h, 1))
            INC(S["ve"])
            P.wait("ve", S["ve"])
            if i >= 1:
                P.wait("ve", S["xnw"], xnw_cnt[i - 1])
            P.emit("ve", lambda v: v.tensor_scalar(
                out=XNS[:], in0=XS[:], scalar1=mcol[:, 0:1], scalar2=mcol[:, 1:2],
                op0=ALU.subtract, op1=ALU.mult).then_inc(S["ve"].h, 1))
            INC(S["ve"])
            ve_xn[i] = S["ve"].n
            P.wait("sy", S["ve"], ve_xn[i])
            P.emit("sy", lambda s, i=i: s.dma_start(
                out=xn_h[ts_(i, 128), :], in_=XNS[:]).then_inc(S["xnw"].h, 16))
            INC(S["xnw"], 16)
            xnw_cnt[i] = S["xnw"].n
            if (i + 1) % 8 == 0:
                P.emit("sy", lambda s, cnt=S["xnw"].n: (
                    s.wait_ge(S["xnw"].h, cnt),
                    s.sem_inc(S["xnb"].h, 1)))
                INC(S["xnb"], 1)
        XNW_ALL = S["xnw"].n

        cache_jobs = []
        for r in range(H * DV // 128):
            cache_jobs.append(("plain", wo, wo_bf[ts_(r, 128), :], r, 0, D))
        for r in range(NDT):
            for c0 in range(0, HID, 2048):
                h0 = c0 // 128
                cache_jobs.append(("blk", wgt,
                                   wgu_bl[h0:h0 + 16, :, r * 256:r * 256 + 128].rearrange("h p c -> p h c"),
                                   r, c0, 2048))
                cache_jobs.append(("blk", wut,
                                   wgu_bl[h0:h0 + 16, :, r * 256 + 128:r * 256 + 256].rearrange("h p c -> p h c"),
                                   r, c0, 2048))
        for hk in range(HT):
            cache_jobs.append(("wd", wdt, None, hk, 0, 2048))
        cache_pos = [0]
        cache_started = [False]

        def emit_cache_jobs(njobs):
            i0 = cache_pos[0]
            for (kind, src, dst_ap, r, c0, cw) in cache_jobs[i0:i0 + njobs]:
                if not cache_started[0]:
                    P.wait("po", S["xnw"], XNW_ALL)
                    cache_started[0] = True
                P.wait("po", S["wcw"])
                P.emit("po", lambda g, src=src, r=r, c0=c0, cw=cw: g.dma_start(
                    out=CSTG[:, 0:cw], in_=src[ts_(r, 128), c0:c0 + cw]).then_inc(S["cld"].h, 16))
                INC(S["cld"], 16)
                P.wait("po", S["cld"])
                if kind == "plain" and src is wo:
                    P.wait("po", S["tab"], TABN)
                    P.emit("po", lambda g, r=r, cw=cw: g.tensor_scalar(
                        out=CSTG[:, 0:cw], in0=CSTG[:, 0:cw],
                        scalar1=gnw_s[:, r % 4:r % 4 + 1], scalar2=None,
                        op0=ALU.mult).then_inc(S["po"].h, 1))
                    INC(S["po"])
                    P.wait("po", S["po"])
                    P.emit("po", lambda a, dst_ap=dst_ap, cw=cw: a.dma_start(
                        out=dst_ap, in_=CSTG[:, 0:cw]).then_inc(S["wcw"].h, 16))
                elif kind == "plain":
                    P.emit("po", lambda a, dst_ap=dst_ap, cw=cw: a.dma_start(
                        out=dst_ap, in_=CSTG[:, 0:cw]).then_inc(S["wcw"].h, 16))
                elif kind == "blk":
                    P.emit("po", lambda a, dst_ap=dst_ap: a.dma_start(
                        out=dst_ap,
                        in_=CSTG[:, 0:2048].rearrange("p (h c) -> p h c", c=128)).then_inc(S["wcw"].h, 16))
                else:
                    P.emit("po", lambda a, hk=r: a.dma_start(
                        out=wd_bl[:, :, ts_(hk, 128)].rearrange("i p c -> p i c"),
                        in_=CSTG[:, 0:2048].rearrange("p (i c) -> p i c", c=128)).then_inc(S["wcw"].h, 16))
                INC(S["wcw"], 16)
            cache_pos[0] = min(len(cache_jobs), i0 + njobs)

        def load_head_qk(h, war_te):
            pb = h % 2
            wl = S["wldA"] if pb == 0 else S["wldB"]
            P.wait("po", S["te"], war_te)
            for w_ap, dstf in ((wq, wq_v), (wk, wk_v)):
                for i0 in range(0, NDT, 4):
                    P.emit("po", lambda g, w_ap=w_ap, dstf=dstf, h=h, pb=pb, i0=i0, wl=wl: g.dma_start(
                        out=dstf(pb).rearrange("p (i c) -> p i c", c=256)[:, i0:i0 + 4, :],
                        in_=w_ap[:, ts_(h, DK)].rearrange("(i p) c -> p i c", p=128)[:, i0:i0 + 4, :],
                    ).then_inc(wl.h, 16))
                    INC(wl, 16)

        def load_head_vg(h, war_te):
            P.wait("po", S["te"], war_te)
            for w_ap, dst in ((wv, WV_V), (wg, WG_V)):
                for i0 in range(0, NDT, 4):
                    P.emit("po", lambda g, w_ap=w_ap, dst=dst, h=h, i0=i0: g.dma_start(
                        out=dst.rearrange("p (i c) -> p i c", c=512)[:, i0:i0 + 4, :],
                        in_=w_ap[:, ts_(h, DV)].rearrange("(i p) c -> p i c", p=128)[:, i0:i0 + 4, :],
                    ).then_inc(S["wldV"].h, 16))
                    INC(S["wldV"], 16)

        def load_head_tabs(h):
            pb = h % 2
            P.emit("sy", lambda s, h=h, pb=pb: s.dma_start(
                out=dth_v(pb), in_=dtab[:, h * 2 * CH:(h + 1) * 2 * CH]).then_inc(S["tabh"].h, 16))
            INC(S["tabh"], 16)
            P.emit("sy", lambda s, h=h, pb=pb: s.dma_start(
                out=dqh_v(pb), in_=dqt[:, ts_(h, CH)]).then_inc(S["tabh"].h, 16))
            INC(S["tabh"], 16)
            P.emit("sy", lambda s, h=h, pb=pb: s.dma_start(
                out=dkh_v(pb), in_=dkt[:, ts_(h, CH)]).then_inc(S["tabh"].h, 16))
            INC(S["tabh"], 16)

        P.emit("ve", lambda v: v.memset(SF[:], 0.0).then_inc(S["ve"].h, 1))
        INC(S["ve"])
        P.wait("te", S["tab"], TABN)
        load_head_qk(0, 0)
        load_head_vg(0, 0)
        qk_done_te = {-1: 0}
        vg_done_te = {-1: 0}
        QDKD_last = 0
        rope_done_ve = 0
        xnt_war_te = {0: 0, 1: 0}
        ogw_prev = 0
        sbf_ac = 0
        dtab_used_ve = 0
        dtab_used_po = 0

        tab_war = {0: 0, 1: 0}
        for h in range(H):
            pb = h % 2
            P.wait("sy", S["ve"], tab_war[pb])
            load_head_tabs(h)
            tab_h = S["tabh"].n
            W_QK = (S["wldA"] if pb == 0 else S["wldB"]).n
            W_VG = S["wldV"].n
            if h + 1 < H:
                load_head_qk(h + 1, qk_done_te[h - 1])
            emit_cache_jobs(32)

            for til in range(NTIL):
                is_pre = til < NPRE
                tpar = til % 2
                XNT = xnt_v(tpar)
                P.wait("sy", S["xnb"], til + 1)
                P.wait("ac", S["xnb"], til + 1)
                P.wait("sy", S["te"], xnt_war_te[tpar])
                P.wait("ac", S["te"], xnt_war_te[tpar])
                xts = "xtA" if tpar == 0 else "xtB"
                xt2s = "xt2A" if tpar == 0 else "xt2B"
                for i_d in range(NDT):
                    sm = xts if i_d % 2 == 0 else xt2s
                    P.emit("sy", lambda s, til=til, i_d=i_d, XNT=XNT, sm=sm: s.dma_start_transpose(
                        XNT[:, ts_(i_d, TILE)],
                        xn_h[ts_(til, TILE), ts_(i_d, 128)]).then_inc(S[sm].h, 16))
                    INC(S[sm], 16)
                XT_N, XT2_N = S[xts].n, S[xt2s].n
                P.wait("te", S[xts], XT_N)
                P.wait("te", S[xt2s], XT2_N)
                P.wait("te", S["tabh"], tab_h)
                P.wait("te", S["ve"])
                P.wait("te", S["ac"])

                def xn_sl(i_d, tc, XNT=XNT):
                    return XNT[:, i_d * TILE + tc * 512: i_d * TILE + (tc + 1) * 512]

                P.wait("te", S["wldA"] if pb == 0 else S["wldB"], W_QK)
                tensors = [("k", wk_v(pb), KP)] if is_pre else \
                          [("q", wq_v(pb), QP), ("k", wk_v(pb), KP)]
                for nm, wsb, dstb in tensors:
                    P.wait("te", S["ve"])
                    def qk_mm(t, wsb=wsb, xn_sl=xn_sl):
                        mm = None
                        for dk_t in range(2):
                            for i_d in range(NDT):
                                wsl = wsb[:, i_d * 256 + dk_t * 128: i_d * 256 + (dk_t + 1) * 128]
                                for tc in range(2):
                                    mm = t.matmul(pf(dk_t * 2 + tc), wsl, xn_sl(i_d, tc),
                                                  start=(i_d == 0), stop=(i_d == NDT - 1))
                        mm.then_inc(S["te"].h, 1)
                    P.emit("te", qk_mm)
                    INC(S["te"])
                    MMQK = S["te"].n
                    P.wait("ve", S["te"], MMQK)
                    for tc in range(2):
                        P.wait("ac", S["ve"], rope_done_ve)
                        P.emit("ac", lambda s, til=til, tc=tc: s.dma_start(
                            out=COS[:], in_=rc_t[:, til * TILE + tc * 512: til * TILE + (tc + 1) * 512]).then_inc(S["rcs"].h, 16))
                        INC(S["rcs"], 16)
                        P.emit("ac", lambda s, til=til, tc=tc: s.dma_start(
                            out=SIN[:], in_=rs_t[:, til * TILE + tc * 512: til * TILE + (tc + 1) * 512]).then_inc(S["rcs"].h, 16))
                        INC(S["rcs"], 16)
                        ROPE_N = S["rcs"].n
                        P.wait("ve", S["rcs"], ROPE_N)
                        for half in range(2):
                            a_b, b_b = (0 + tc, 2 + tc) if half == 0 else (2 + tc, 0 + tc)
                            P.wait("ve", S["ve"])
                            P.emit("ve", lambda v, a_b=a_b, b_b=b_b: (
                                v.tensor_tensor(RT[:, 0:512], pf(a_b), COS[:], ALU.mult),
                                v.tensor_tensor(RT[:, 512:1024], pf(b_b), SIN[:], ALU.mult).then_inc(S["ve"].h, 1)))
                            INC(S["ve"])
                            P.wait("ve", S["ve"])
                            op = ALU.subtract if half == 0 else ALU.add
                            P.emit("ve", lambda v, tc=tc, dstb=dstb, half=half, op=op: v.tensor_tensor(
                                dstb[:, half * TILE + tc * 512: half * TILE + (tc + 1) * 512],
                                RT[:, 0:512], RT[:, 512:1024], op).then_inc(S["ve"].h, 1))
                            INC(S["ve"])
                            rope_done_ve = S["ve"].n
                qk_rope_ve = S["ve"].n
                if not is_pre:
                    qk_done_te[h] = S["te"].n

                P.wait("te", S["wldV"], W_VG)
                vdone_ve = {}
                gdone_ac = {}
                for tt8 in range(8):
                    vb = 4 + (tt8 % 2)
                    if tt8 >= 2:
                        P.wait("te", S["ve"], vdone_ve[tt8 - 2])
                    if tt8 >= 1 and not is_pre:
                        P.wait("te", S["ac"], gdone_ac[tt8 - 1])
                    def vg_mm(t, tt8=tt8, vb=vb, is_pre=is_pre, XNT=XNT):
                        mm = None
                        for i_d in range(NDT):
                            xsl = XNT[:, i_d * TILE + tt8 * 128: i_d * TILE + (tt8 + 1) * 128]
                            mm = t.matmul(pf(vb), xsl, WV_V[:, ts_(i_d, 512)],
                                          start=(i_d == 0), stop=(i_d == NDT - 1))
                            if not is_pre:
                                mm = t.matmul(pf(6), xsl, WG_V[:, ts_(i_d, 512)],
                                              start=(i_d == 0), stop=(i_d == NDT - 1))
                        mm.then_inc(S["te"].h, 1)
                    P.emit("te", vg_mm)
                    INC(S["te"])
                    MMV = S["te"].n
                    P.wait("ve", S["te"], MMV)
                    P.emit("ve", lambda v, tt8=tt8, vb=vb: v.tensor_copy(
                        VT[:, ts_(tt8, 512)], pf(vb)).then_inc(S["ve"].h, 1))
                    INC(S["ve"])
                    vdone_ve[tt8] = S["ve"].n
                    if not is_pre:
                        P.wait("ac", S["te"], MMV)
                        P.emit("ac", lambda a, tt8=tt8: a.activation(
                            GS[:, ts_(tt8, 512)], pf(6), AF.Silu).then_inc(S["ac"].h, 1))
                        INC(S["ac"])
                        gdone_ac[tt8] = S["ac"].n
                vg_mm_last = S["te"].n
                xnt_war_te[tpar] = vg_mm_last
                vg_done_te[h] = vg_mm_last
                V_DONE = vdone_ve[7]
                G_DONE = gdone_ac.get(7, 0)

                for c in range(NCH):
                    gch = til * NCH + c
                    par = gch % 2
                    P.wait("ve", S["tabh"], tab_h)
                    P.wait("ve", S["te"])
                    if not is_pre:
                        P.emit("ve", lambda v, c=c, pb=pb: (
                            v.tensor_tensor(QDC[:, 0:256], QP[:, c * CH:(c + 1) * CH], dqh_v(pb)[:], ALU.mult),
                            v.tensor_tensor(QDC[:, 256:512], QP[:, TILE + c * CH:TILE + (c + 1) * CH], dqh_v(pb)[:], ALU.mult).then_inc(S["ve"].h, 1)))
                        INC(S["ve"])
                    P.emit("ve", lambda v, c=c, pb=pb: (
                        v.tensor_tensor(KDC[:, 0:256], KP[:, c * CH:(c + 1) * CH], dkh_v(pb)[:], ALU.mult),
                        v.tensor_tensor(KDC[:, 256:512], KP[:, TILE + c * CH:TILE + (c + 1) * CH], dkh_v(pb)[:], ALU.mult).then_inc(S["ve"].h, 1)))
                    INC(S["ve"])
                    QDKD = S["ve"].n
                    QDKD_last = QDKD

                    P.wait("te", S["ve"], QDKD)
                    P.wait("te", S["ac"])
                    def kdt_mm(t):
                        mm = None
                        for j_t in range(2):
                            for dk_t in range(2):
                                mm = t.transpose(ps_b[:, (j_t * 2 + dk_t) * 128:(j_t * 2 + dk_t + 1) * 128],
                                                 KDC[:, dk_t * 256 + j_t * 128: dk_t * 256 + (j_t + 1) * 128],
                                                 id_s[:])
                        mm.then_inc(S["te"].h, 1)
                    P.emit("te", kdt_mm)
                    INC(S["te"])
                    P.wait("ac", S["te"], S["te"].n)
                    P.emit("ac", lambda a: a.copy(
                        kdT_s[:, 0:512], ps_b[:, 0:512]).then_inc(S["ac"].h, 1))
                    INC(S["ac"])
                    KDT_AC = S["ac"].n

                    O_MM = None
                    if not is_pre:
                        P.wait("te", S["ac"], G_DONE)
                        P.wait("te", S["ve"])
                        def sc_mm(t, c=c):
                            mm = None
                            for j_t in range(2):
                                for dk_t in range(2):
                                    mm = t.matmul(pf(6)[:, j_t * 256:(j_t + 1) * 256],
                                                  KP[:, dk_t * TILE + c * CH + j_t * 128: dk_t * TILE + c * CH + (j_t + 1) * 128],
                                                  QP[:, dk_t * TILE + c * CH: dk_t * TILE + (c + 1) * CH],
                                                  start=(dk_t == 0), stop=(dk_t == 1))
                            mm.then_inc(S["te"].h, 1)
                        P.emit("te", sc_mm)
                        INC(S["te"])
                        SC_MM = S["te"].n
                        P.wait("ve", S["te"], SC_MM)
                        P.emit("ve", lambda v, pb=pb: (
                            v.tensor_tensor(mt_s[:, 0:256], pf(6)[:, 0:256], dth_v(pb)[:, 0:256], ALU.mult),
                            v.tensor_tensor(mt_s[:, 256:512], pf(6)[:, 256:512], dth_v(pb)[:, 256:512], ALU.mult).then_inc(S["ve"].h, 1)))
                        INC(S["ve"])
                        MT_VE = S["ve"].n
                        dtab_used_ve = MT_VE

                    P.wait("te", S["ac"], KDT_AC)
                    P.wait("te", S["ve"], V_DONE)
                    P.wait("te", S["ve"])
                    def st_mm(t, c=c):
                        mm = None
                        for dk_t in range(2):
                            for j_t in range(2):
                                mm = t.matmul(pf(4 + dk_t),
                                              kdT_s[:, (j_t * 2 + dk_t) * 128: (j_t * 2 + dk_t + 1) * 128],
                                              VT[:, ts_(c * 2 + j_t, 512)],
                                              start=(j_t == 0), stop=(j_t == 1))
                        mm.then_inc(S["te"].h, 1)
                    P.emit("te", st_mm)
                    INC(S["te"])
                    ST_MM = S["te"].n

                    if not is_pre:
                        P.wait("te", S["ve"], MT_VE)
                        P.wait("te", S["ac"])
                        def o_mm(t, c=c, par=par):
                            sprev = sbf_v(1 - par)
                            mm = None
                            for i_t in range(2):
                                for j_t in range(2):
                                    t.matmul(pf(0 + i_t),
                                             mt_s[:, j_t * 256 + i_t * 128: j_t * 256 + (i_t + 1) * 128],
                                             VT[:, ts_(c * 2 + j_t, 512)],
                                             start=(j_t == 0), stop=False)
                                for dk_t in range(2):
                                    mm = t.matmul(pf(0 + i_t),
                                                  QDC[:, dk_t * 256 + i_t * 128: dk_t * 256 + (i_t + 1) * 128],
                                                  sprev[:, ts_(dk_t, 512)],
                                                  start=False, stop=(dk_t == 1))
                            mm.then_inc(S["te"].h, 1)
                        P.emit("te", o_mm)
                        INC(S["te"])
                        O_MM = S["te"].n

                    gc = float(GAM[h] ** CH)
                    P.wait("ve", S["te"], ST_MM)
                    P.wait("ve", S["ve"])
                    P.emit("ve", lambda v, gc=gc: (
                        v.scalar_tensor_tensor(out=SF[:, 0:512], in0=SF[:, 0:512], scalar=gc,
                                               in1=pf(4), op0=ALU.mult, op1=ALU.add),
                        v.scalar_tensor_tensor(out=SF[:, 512:1024], in0=SF[:, 512:1024], scalar=gc,
                                               in1=pf(5), op0=ALU.mult, op1=ALU.add).then_inc(S["ve"].h, 1)))
                    INC(S["ve"])
                    STT_VE = S["ve"].n
                    if not is_pre:
                        P.wait("ac", S["ve"], STT_VE)
                        P.wait("ac", S["te"], O_MM)
                        P.emit("ac", lambda a, par=par: a.copy(
                            sbf_v(par), SF[:]).then_inc(S["ac"].h, 1))
                        INC(S["ac"])
                        sbf_ac = S["ac"].n

                        P.wait("ve", S["te"], O_MM)
                        P.wait("ve", S["ac"], G_DONE)
                        if c == 0:
                            P.wait("ve", S["ogw"], ogw_prev)
                        P.emit("ve", lambda v, c=c: (
                            v.tensor_tensor(OGB[:, ts_(c * 2, 512)], pf(0), GS[:, ts_(c * 2, 512)], ALU.mult),
                            v.tensor_tensor(OGB[:, ts_(c * 2 + 1, 512)], pf(1), GS[:, ts_(c * 2 + 1, 512)], ALU.mult).then_inc(S["ve"].h, 1)))
                        INC(S["ve"])
                        OS_VE = S["ve"].n
                        P.wait("ac", S["ve"], OS_VE)
                        P.emit("ac", lambda a, c=c: (
                            a.activation(pf(0), pf(0), AF.Square,
                                         accum_out=ms_s[:, c * 2:c * 2 + 1]),
                            a.activation(pf(1), pf(1), AF.Square,
                                         accum_out=ms_s[:, c * 2 + 1:c * 2 + 2]).then_inc(S["ac"].h, 1)))
                        INC(S["ac"])

                if is_pre and til == NPRE - 1:
                    P.wait("ve", S["tab"], TABN)
                    P.wait("ve", S["ve"])
                    P.emit("ve", lambda v: (
                        v.tensor_scalar(out=SF[:, 0:512], in0=SF[:, 0:512],
                                        scalar1=mask_s[:, 0:1], scalar2=None, op0=ALU.mult),
                        v.tensor_scalar(out=SF[:, 512:1024], in0=SF[:, 512:1024],
                                        scalar1=mask_s[:, 0:1], scalar2=None, op0=ALU.mult).then_inc(S["ve"].h, 1)))
                    INC(S["ve"])
                    P.wait("ac", S["ve"])
                    P.emit("ac", lambda a: a.copy(sbf_v(1), SF[:]).then_inc(S["ac"].h, 1))
                    INC(S["ac"])
                    sbf_ac = S["ac"].n

                if not is_pre:
                    AC_MS = S["ac"].n
                    P.wait("ac", S["ac"], AC_MS)
                    P.wait("ac", S["ve"])
                    P.emit("ac", lambda a: a.activation(
                        rstd_s[:], ms_s[:], AF.Sqrt, bias=epsc[:, 0:1],
                        scale=float(1.0 / DV)).then_inc(S["ac"].h, 1))
                    INC(S["ac"])
                    P.wait("ve", S["ac"])
                    P.emit("ve", lambda v: v.reciprocal(rstd_s[:], rstd_s[:]).then_inc(S["ve"].h, 1))
                    INC(S["ve"])
                    P.wait("ve", S["ve"])
                    def og_scale(v):
                        mm = None
                        for sub in range(8):
                            mm = v.tensor_scalar(out=OGB[:, ts_(sub, 512)], in0=OGB[:, ts_(sub, 512)],
                                                 scalar1=rstd_s[:, sub:sub + 1], scalar2=None,
                                                 op0=ALU.mult)
                        mm.then_inc(S["ve"].h, 1)
                    P.emit("ve", og_scale)
                    INC(S["ve"])
                    OG_VE = S["ve"].n
                    P.wait("po", S["ve"], OG_VE)
                    t0 = (til - NPRE) * TILE
                    P.emit("po", lambda s, h=h, t0=t0: s.dma_start(
                        out=og_h[h, t0:t0 + TILE, :].rearrange("(a p) c -> p a c", p=128),
                        in_=OGB.rearrange("p (a c) -> p a c", c=512)).then_inc(S["ogw"].h, 16))
                    INC(S["ogw"], 16)
                    ogw_prev = S["ogw"].n
            if h + 1 < H:
                load_head_vg(h + 1, vg_done_te[h])
            P.wait("ve", S["ve"])
            P.emit("ve", lambda v: v.memset(SF[:], 0.0).then_inc(S["ve"].h, 1))
            INC(S["ve"])
            tab_war[pb] = max(dtab_used_ve, QDKD_last)
        emit_cache_jobs(len(cache_jobs))
        CACHE_DONE = S["wcw"].n
        P1_END = dict(te=S["te"].n, ve=S["ve"].n, ac=S["ac"].n, po=S["po"].n,
                      ogw=S["ogw"].n)

        for e in ("sy", "ac", "po", "te", "ve"):
            P.wait(e, S["te"], P1_END["te"])
            P.wait(e, S["ve"], P1_END["ve"])
            P.wait(e, S["ac"], P1_END["ac"])
            P.wait(e, S["po"], P1_END["po"])
        P.wait("sy", S["ogw"], P1_END["ogw"])
        P.wait("ac", S["ogw"], P1_END["ogw"])
        P.wait("sy", S["wcw"], CACHE_DONE)
        P.wait("ac", S["wcw"], CACHE_DONE)

        fst2_last = {i: None for i in range(4)}
        wo_done = {}
        ogt_sems = (P.sem("ogt"), P.sem("ogtB"))
        for ep in range(2):
            if ep > 0:
                P.wait("sy", S["te"])
                P.wait("ac", S["te"])
                P.wait("sy", S["po"])
                P.wait("ac", S["po"])
            for hv_t in range(32):
                qe = "ac" if hv_t % 2 == 0 else "po"
                P.emit(qe, lambda s, hv_t=hv_t, ep=ep: s.dma_start(
                    out=WOH[:, ts_(hv_t, 1024)],
                    in_=wo_bf[ts_(hv_t, 128), ts_(ep, 1024)]).then_inc(S["wo2"].h, 16))
                INC(S["wo2"], 16)
            WO_LD = S["wo2"].n

            for tg in range(8):
                par = tg % 2
                osem = ogt_sems[par]
                ogv = OGT[:, par * 16384:(par + 1) * 16384]
                P.wait("sy", S["te"], wo_done.get((ep, tg - 2), 0))
                if ep > 0 and tg < 2:
                    P.wait("sy", S["te"], wo_done.get((ep - 1, 6 + tg), 0))
                for hv_t in range(32):
                    hh, dv0 = hv_t // 4, (hv_t % 4) * 128
                    P.emit("sy", lambda s, hh=hh, dv0=dv0, tg=tg, hv_t=hv_t, ogv=ogv, osem=osem: s.dma_start_transpose(
                        ogv[:, ts_(hv_t, 512)],
                        og_h[hh, ts_(tg, 512), dv0:dv0 + 128]).then_inc(osem.h, 16))
                    INC(osem, 16)
                OGT_LD = osem.n
                P.wait("te", osem, OGT_LD)
                P.wait("te", S["wo2"], WO_LD)
                evac_ve = {}
                for et in range(8):
                    bk = et % 4
                    if et >= 4:
                        P.wait("te", S["ve"], evac_ve[et - 4])
                    def wo_mm(t, et=et, bk=bk, ogv=ogv):
                        mm = None
                        for hv_t in range(32):
                            wsl = WOH[:, hv_t * 1024 + et * 128: hv_t * 1024 + (et + 1) * 128]
                            mm = t.matmul(pf(bk), wsl, ogv[:, ts_(hv_t, 512)],
                                          start=(hv_t == 0), stop=(hv_t == 31))
                        mm.then_inc(S["te"].h, 1)
                    P.emit("te", wo_mm)
                    INC(S["te"])
                    WOMM = S["te"].n
                    slot = et % 4
                    aw = S["atwA"] if et % 2 == 0 else S["atwB"]
                    P.wait("ve", S["te"], WOMM)
                    if fst2_last[slot] is not None:
                        P.wait("ve", fst2_last[slot][0], fst2_last[slot][1])
                    P.emit("ve", lambda v, bk=bk, slot=slot: v.tensor_copy(
                        FST2[:, slot * 512:(slot + 1) * 512], pf(bk)).then_inc(S["ve"].h, 1))
                    INC(S["ve"])
                    evac_ve[et] = S["ve"].n
                    P.wait("po", S["ve"], S["ve"].n)
                    P.emit("po", lambda s, ep=ep, et=et, tg=tg, slot=slot, aw=aw: s.dma_start(
                        out=at_h[ep * 1024 + et * 128: ep * 1024 + (et + 1) * 128, ts_(tg, 512)],
                        in_=FST2[:, slot * 512:(slot + 1) * 512]).then_inc(aw.h, 16))
                    INC(aw, 16)
                    fst2_last[slot] = (aw, aw.n)
                wo_done[(ep, tg)] = S["te"].n
        P2_END = dict(te=S["te"].n, ve=S["ve"].n, po=S["po"].n,
                      atwA=S["atwA"].n, atwB=S["atwB"].n)

        for e in ("sy", "ac", "po", "te", "ve"):
            P.wait(e, S["te"], P2_END["te"])
            P.wait(e, S["ve"], P2_END["ve"])
        P.wait("sy", S["atwA"], P2_END["atwA"])
        P.wait("ac", S["atwA"], P2_END["atwA"])
        P.wait("sy", S["atwB"], P2_END["atwB"])
        P.wait("ac", S["atwB"], P2_END["atwB"])

        fw_war_te = {0: 0, 1: 0}
        xnf_war_te = {0: 0, 1: 0}
        outw_after = {}
        po_add_after = {}
        for ft in range(NFT):
            t0 = TP + ft * TILE
            for tc in range(2):
                P.wait("sy", S["te"], xnf_war_te[tc])
                P.wait("ac", S["te"], xnf_war_te[tc])
                xts = "xtA" if tc == 0 else "xtB"
                xt2s = "xt2A" if tc == 0 else "xt2B"
                for i_d in range(NDT):
                    sm = xts if i_d % 2 == 0 else xt2s
                    P.emit("sy", lambda s, i_d=i_d, t0=t0, tc=tc, sm=sm: s.dma_start_transpose(
                        xnf_v(tc)[:, ts_(i_d, 512)],
                        xn_h[t0 + tc * 512: t0 + (tc + 1) * 512, ts_(i_d, 128)]).then_inc(S[sm].h, 16))
                    INC(S[sm], 16)
            XNF_N = (S["xtA"].n, S["xtB"].n, S["xt2A"].n, S["xt2B"].n)

            hid_ve = {}
            silu_ac = {}
            for h_t in range(HT):
                pbw = h_t % 2
                fwsem = S["fwA"] if pbw == 0 else S["fwB"]
                qe = "sy" if h_t % 2 == 0 else "ac"
                P.wait(qe, S["te"], fw_war_te[pbw])
                P.emit(qe, lambda s, h_t=h_t, pbw=pbw, fwsem=fwsem: s.dma_start(
                    out=wdb_v(pbw), in_=wgu_bl[h_t]).then_inc(fwsem.h, 16))
                INC(fwsem, 16)
                FW_LD = fwsem.n
                P.wait("te", fwsem, FW_LD)
                P.wait("te", S["xtA"], XNF_N[0])
                P.wait("te", S["xtB"], XNF_N[1])
                P.wait("te", S["xt2A"], XNF_N[2])
                P.wait("te", S["xt2B"], XNF_N[3])
                if h_t >= 1:
                    P.wait("te", S["ac"], silu_ac.get(h_t - 1, 0))
                def f1_gate(t, pbw=pbw):
                    mm = None
                    wb = wdb_v(pbw)
                    for tc in range(2):
                        for i_d in range(NDT):
                            mm = t.matmul(pf(0 + tc), wb[:, i_d * 256:i_d * 256 + 128],
                                          xnf_v(tc)[:, ts_(i_d, 512)],
                                          start=(i_d == 0), stop=(i_d == NDT - 1))
                        mm.then_inc(S["te"].h, 1)
                P.emit("te", f1_gate)
                INC(S["te"], 2)
                G_TC = (S["te"].n - 1, S["te"].n)
                if h_t >= 1:
                    P.wait("te", S["ve"], hid_ve[(h_t - 1, 1)])
                def f1_up(t, pbw=pbw):
                    mm = None
                    wb = wdb_v(pbw)
                    for tc in range(2):
                        for i_d in range(NDT):
                            mm = t.matmul(pf(2 + tc), wb[:, i_d * 256 + 128:i_d * 256 + 256],
                                          xnf_v(tc)[:, ts_(i_d, 512)],
                                          start=(i_d == 0), stop=(i_d == NDT - 1))
                        mm.then_inc(S["te"].h, 1)
                P.emit("te", f1_up)
                INC(S["te"], 2)
                U_TC = (S["te"].n - 1, S["te"].n)
                F1MM = S["te"].n
                fw_war_te[pbw] = F1MM
                for tc in range(2):
                    P.wait("ac", S["te"], G_TC[tc])
                    P.wait("ac", S["ve"],
                           hid_ve[(h_t - 1, 1)] if (tc == 0 and h_t >= 1)
                           else (hid_ve[(h_t, 0)] if tc == 1 else 0))
                    P.emit("ac", lambda a, tc=tc: a.activation(
                        sgq[:], pf(0 + tc), AF.Silu).then_inc(S["ac"].h, 1))
                    INC(S["ac"])
                    if tc == 1:
                        silu_ac[h_t] = S["ac"].n
                    P.wait("ve", S["ac"], S["ac"].n)
                    P.wait("ve", S["te"], U_TC[tc])
                    P.emit("ve", lambda v, h_t=h_t, tc=tc: v.tensor_tensor(
                        HIDB[:, h_t * 1024 + tc * 512: h_t * 1024 + (tc + 1) * 512],
                        sgq[:], pf(2 + tc), ALU.mult).then_inc(S["ve"].h, 1))
                    INC(S["ve"])
                    hid_ve[(h_t, tc)] = S["ve"].n
            HID_VE = S["ve"].n
            for tc in range(2):
                xnf_war_te[tc] = S["te"].n
            if debug and ft == 0:
                P.wait("sy", S["ve"], HID_VE)
                P.emit("sy", lambda s: s.dma_start(out=hid_dbg, in_=HIDB).then_inc(S["fr"].h, 16))
                INC(S["fr"], 16)

            P.wait("te", S["ve"], HID_VE)
            dt_evac_ve = {}
            for d_t in range(NDT):
                bk = (d_t % 2) * 2
                fw_lds = []
                for hf in range(2):
                    slot = (d_t * 2 + hf) % 2
                    fwsem = S["fwA"] if slot == 0 else S["fwB"]
                    qe = "sy" if hf == 0 else "ac"
                    P.wait(qe, S["te"], fw_war_te[slot])
                    P.emit(qe, lambda s, d_t=d_t, hf=hf, slot=slot, fwsem=fwsem: s.dma_start(
                        out=wdb_v(slot),
                        in_=wd_bl[d_t, :, hf * 4096:(hf + 1) * 4096]).then_inc(fwsem.h, 16))
                    INC(fwsem, 16)
                    fw_lds.append((fwsem, fwsem.n))
                qe = "ac"
                if d_t >= 1:
                    P.wait(qe, S["ve"], dt_evac_ve[d_t - 1])
                    P.wait(qe, S["po"], po_add_after[d_t - 1])
                P.emit(qe, lambda s, d_t=d_t, ft=ft: s.dma_start(
                    out=FST4[:, 0:1024], in_=at_h[ts_(d_t, 128), ts_(ft, TILE)]).then_inc(S["fr"].h, 16))
                INC(S["fr"], 16)
                P.emit(qe, lambda s, d_t=d_t, ft=ft: s.dma_start(
                    out=FST4[:, 1024:2048], in_=x_fm[ts_(d_t, 128), ts_(ft, TILE)]).then_inc(S["fr"].h, 16))
                INC(S["fr"], 16)
                FR_LD = S["fr"].n
                if d_t >= 2:
                    P.wait("te", S["ve"], dt_evac_ve[d_t - 2])
                for hf in range(2):
                    slot = (d_t * 2 + hf) % 2
                    P.wait("te", fw_lds[hf][0], fw_lds[hf][1])
                    def f2_mm(t, hf=hf, slot=slot, bk=bk):
                        wb = wdb_v(slot)
                        mm = None
                        for hk in range(32):
                            wsl = wb[:, ts_(hk, 128)]
                            hkg = hf * 32 + hk
                            for tc in range(2):
                                mm = t.matmul(pf(bk + tc), wsl,
                                              HIDB[:, hkg * 1024 + tc * 512: hkg * 1024 + (tc + 1) * 512],
                                              start=(hkg == 0), stop=(hkg == HT - 1))
                        mm.then_inc(S["te"].h, 1)
                    P.emit("te", f2_mm)
                    INC(S["te"])
                    fw_war_te[slot] = S["te"].n
                F2MM = S["te"].n
                P.wait("ve", S["te"], F2MM)
                P.wait("ve", S["fr"], FR_LD)
                if d_t >= 1:
                    P.wait("ve", S["po"], po_add_after[d_t - 1])
                P.emit("ve", lambda v, bk=bk: (
                    v.tensor_tensor(RT4[:, 0:512], pf(bk), FST4[:, 0:512], ALU.add),
                    v.tensor_tensor(RT4[:, 512:1024], pf(bk + 1), FST4[:, 512:1024], ALU.add).then_inc(S["ve"].h, 1)))
                INC(S["ve"])
                dt_evac_ve[d_t] = S["ve"].n
                P.wait("po", S["ve"], S["ve"].n)
                key = (ft, d_t - 1)
                if key in outw_after:
                    P.wait("po", S["outw"], outw_after[key])
                elif ft > 0:
                    P.wait("po", S["outw"], outw_after[(ft - 1, NDT - 1)])
                P.emit("po", lambda g: (
                    g.tensor_tensor(RT4[:, 1024:1536], RT4[:, 0:512], FST4[:, 1024:1536], ALU.add),
                    g.tensor_tensor(RT4[:, 1536:2048], RT4[:, 512:1024], FST4[:, 1536:2048], ALU.add).then_inc(S["po"].h, 1)))
                INC(S["po"])
                po_add_after[d_t] = S["po"].n
                P.emit("po", lambda g, d_t=d_t, ft=ft: g.dma_start(
                    out=out_fm[ts_(d_t, 128), ts_(ft, TILE)],
                    in_=RT4[:, 1024:2048]).then_inc(S["outw"].h, 16))
                INC(S["outw"], 16)
                outw_after[(ft, d_t)] = S["outw"].n

        P.wait("sy", S["outw"], S["outw"].n)

        for name in P.sems:
            P.sems[name].h = E(nc.semaphore(name))
        blk = E(nc.Block())

        @blk.tensor
        def _(t):
            for fn in P.q["te"]:
                fn(t)

        @blk.vector
        def _(v):
            for fn in P.q["ve"]:
                fn(v)

        @blk.scalar
        def _(a):
            for fn in P.q["ac"]:
                fn(a)

        @blk.gpsimd
        def _(g):
            for fn in P.q["po"]:
                fn(g)

        @blk.sync
        def _(s):
            for fn in P.q["sy"]:
                fn(s)

    return nc


def _host_constants():
    j = np.arange(128, dtype=np.float64)
    inv_freq = 1.0 / (10000.0 ** (2.0 * j / DK))
    consts = {}
    for half in range(2):
        pos_main = np.arange(half * TM, half * TM + TM, dtype=np.float64)
        if half == 1:
            pos = np.concatenate([np.arange(0, TP, dtype=np.float64), pos_main])
        else:
            pos = np.concatenate([pos_main, pos_main])
        ang = pos[None, :] * inv_freq[:, None]
        consts[(half, "cos")] = np.ascontiguousarray(np.cos(ang).astype(np.float32))
        consts[(half, "sin")] = np.ascontiguousarray(np.sin(ang).astype(np.float32))
    ii = np.arange(CH, dtype=np.float64)
    dtab_ = np.zeros((128, H * 2 * CH), dtype=np.float32)
    dq = np.zeros((128, H * CH), dtype=np.float32)
    dk = np.zeros((128, H * CH), dtype=np.float32)
    for h in range(H):
        g = GAM[h]
        for j_t in range(2):
            jj = np.arange(128, dtype=np.float64)[:, None] + j_t * 128
            diff = ii[None, :] - jj
            m = (diff >= 0).astype(np.float64)
            dtab_[:, (h * 2 + j_t) * CH:(h * 2 + j_t + 1) * CH] = \
                (np.power(g, np.maximum(diff, 0.0)) * m * QSC).astype(np.float32)
        dq[:, h * CH:(h + 1) * CH] = np.broadcast_to(
            (np.power(g, ii + 1.0) * QSC).astype(np.float32), (128, CH))
        dk[:, h * CH:(h + 1) * CH] = np.broadcast_to(
            np.power(g, CH - 1.0 - ii).astype(np.float32), (128, CH))
    consts["dtab"] = dtab_.astype(ml_dtypes.bfloat16)
    consts["dq_bf"] = dq.astype(ml_dtypes.bfloat16)
    consts["dk_bf"] = dk.astype(ml_dtypes.bfloat16)
    consts["ident"] = np.eye(128, dtype=np.float32).astype(ml_dtypes.bfloat16)
    return consts


_NC_CACHE = {}

def _get_nc():
    if "nc" not in _NC_CACHE:
        _NC_CACHE["nc"] = build_program()
    return _NC_CACHE["nc"]


def kernel(L0_attn_hidden_states, L0_attn_Wq, L0_attn_Wk, L0_attn_Wv,
           L0_attn_Wg, L0_attn_Wo, L0_attn_g_norm_weight,
           L0_ffn_w_gate, L0_ffn_w_up, L0_ffn_w_down):
    x = np.asarray(L0_attn_hidden_states, dtype=np.float32)
    consts = _host_constants()
    gn = np.asarray(L0_attn_g_norm_weight, np.float32)
    gnwc = np.ascontiguousarray(gn.reshape(DV // 128, 128).T)

    wgt_a = np.ascontiguousarray(np.asarray(L0_ffn_w_gate, np.float32).T)
    wut_a = np.ascontiguousarray(np.asarray(L0_ffn_w_up, np.float32).T)
    wdt_a = np.ascontiguousarray(np.asarray(L0_ffn_w_down, np.float32).T)

    shared = {
        "wq": np.ascontiguousarray(np.asarray(L0_attn_Wq, np.float32)),
        "wk": np.ascontiguousarray(np.asarray(L0_attn_Wk, np.float32)),
        "wv": np.ascontiguousarray(np.asarray(L0_attn_Wv, np.float32)),
        "wg": np.ascontiguousarray(np.asarray(L0_attn_Wg, np.float32)),
        "wo": np.ascontiguousarray(np.asarray(L0_attn_Wo, np.float32)),
        "wgt": wgt_a, "wut": wut_a, "wdt": wdt_a,
        "dtab": consts["dtab"], "dqt": consts["dq_bf"], "dkt": consts["dk_bf"],
        "gnwc": gnwc, "identt": consts["ident"],
    }
    in_maps = []
    for c in range(8):
        b, half = c // 2, c % 2
        xb = x[b]
        main = xb[half * TM:(half + 1) * TM]
        pre = xb[0:TP] if half == 1 else main
        m = dict(shared)
        m["x_tm"] = np.ascontiguousarray(np.concatenate([pre, main], axis=0))
        m["x_fm"] = np.ascontiguousarray(main.T)
        m["maskc"] = np.full((128, 1), 1.0 if half == 1 else 0.0, np.float32)
        m["ropec"] = consts[(half, "cos")]
        m["ropes"] = consts[(half, "sin")]
        in_maps.append(m)
    nc = _get_nc()
    res = run_bass_kernel_spmd(nc, in_maps, core_ids=list(range(8)))
    out = np.empty((B, T, D), dtype=np.float32)
    for c in range(8):
        b, half = c // 2, c % 2
        out[b, half * TM:(half + 1) * TM, :] = res.results[c]["out_fm"].T
    return out
